# version 10
# baseline (speedup 1.0000x reference)
import numpy as np
import concourse.bass as bass
import concourse.mybir as mybir
from concourse.bass_utils import run_bass_kernel_spmd

F32 = mybir.dt.float32
AF = mybir.ActivationFunctionType
AX = mybir.AxisListType
AL = mybir.AluOpType

B, N, DM = 4, 2048, 2048
H, DH = 16, 128
BLK = 256
TPC = 1024
NBLK = TPC // BLK
NM = TPC // 128
G4 = 4 * DM
E, I = 32, 768
NIC = I // 128
LN_EPS = 1e-5
RMS_EPS = 1e-6
SPARSE = True
CAP = 192

ENGS = ("sync", "gpsimd", "vector", "scalar", "tensor")
SEM_PROD = {"dW": "sync", "dX": "gpsimd", "tM": "tensor", "vV": "vector",
            "aA": "scalar", "gG": "gpsimd"}
ASYNC_SEMS = ("dW", "dX")


class Rec:

    def __init__(self):
        self.rec = {e: [] for e in ENGS}
        self.cnt = {s: 0 for s in SEM_PROD}
        self.waited = {}

    def wait(self, eng, sem, val=None, force=False):
        if val is None:
            val = self.cnt[sem]
        if val <= 0:
            return
        if SEM_PROD[sem] == eng and sem not in ASYNC_SEMS and not force:
            return
        key = (eng, sem)
        if self.waited.get(key, 0) >= val:
            return
        self.waited[key] = val
        self.rec[eng].append(lambda EE, SS, _e=eng, _s=sem, _v=val: EE[_e].wait_ge(SS[_s], _v))

    def ins(self, eng, fn, sem=None, d=1):
        if sem is not None:
            self.cnt[sem] += d
        def go(EE, SS, _fn=fn, _sem=sem, _d=d, _eng=eng):
            rr = _fn(EE[_eng])
            if _sem is not None:
                rr.then_inc(SS[_sem], _d)
        self.rec[eng].append(go)
        return self.cnt[sem] if sem is not None else None

    def v(self, fn):
        self.wait("vector", "vV", force=True)
        return self.ins("vector", fn, "vV")

    def a(self, fn):
        self.wait("scalar", "aA", force=True)
        return self.ins("scalar", fn, "aA")

    def g(self, fn):
        self.wait("gpsimd", "gG", force=True)
        return self.ins("gpsimd", fn, "gG")

    def t(self, fn, inc=False):
        return self.ins("tensor", fn, "tM" if inc else None)

    def dW(self, fn):
        self.wait("sync", "dW", force=True)
        return self.ins("sync", fn, "dW", 16)

    def dX(self, fn):
        self.wait("gpsimd", "dX", force=True)
        return self.ins("gpsimd", fn, "dX", 16)

    def barrier(self):
        for eng in ENGS:
            for sem in SEM_PROD:
                self.wait(eng, sem)


def ceil128(x):
    return (x + 127) // 128 * 128


def build_nc(debug=False):
    nc = bass.Bass("TRN2", target_bir_lowering=False)

    x_own_d = nc.dram_tensor("x_own", [TPC, DM], F32, kind="ExternalInput").ap()
    x_pre_d = nc.dram_tensor("x_pre", [TPC, DM], F32, kind="ExternalInput").ap()
    slope_d = nc.dram_tensor("slope", [H], F32, kind="ExternalInput").ap()
    wqkvuT_d = nc.dram_tensor("wqkvuT", [DM, G4], F32, kind="ExternalInput").ap()
    woutT_d = nc.dram_tensor("woutT", [DM, DM], F32, kind="ExternalInput").ap()
    wrT_d = nc.dram_tensor("wrT", [DM, E], F32, kind="ExternalInput").ap()
    wguT_d = nc.dram_tensor("wguT", [E, DM, 2 * I], F32, kind="ExternalInput").ap()
    wdwT_d = nc.dram_tensor("wdwT", [E, I, DM], F32, kind="ExternalInput").ap()
    out_d = nc.dram_tensor("out", [TPC, DM], F32, kind="ExternalOutput").ap()
    s_spill_d = nc.dram_tensor("s_spill", [TPC, DM], F32, kind="Internal").ap()
    xs_d = nc.dram_tensor("xs_slots", [E * CAP, DM], F32, kind="Internal").ap()
    ys_d = nc.dram_tensor("ys_slots", [E * CAP, DM], F32, kind="Internal").ap()
    xn_d = nc.dram_tensor("xn_rows", [TPC, DM], F32, kind="Internal").ap()
    dbg = {}
    if debug:
        for name, shape in (("xnT_dump", [16, 128, TPC]), ("kv_dump", [128, H * DH]),
                            ("s_dump", [TPC, DM]), ("wtok_dump", [NM, 128, E]),
                            ("kvblk_dump", [NBLK, 128, H * DH]),
                            ("p0dbg", [4, 128, 512])):
            dbg[name] = nc.dram_tensor(name, shape, F32, kind="ExternalOutput").ap()

    OFF_XNT = 0
    SZ_XNT = 16 * TPC
    OFF_KV = OFF_XNT + SZ_XNT
    SZ_KV = H * DH
    OFF_WTOK = OFF_KV + SZ_KV
    OFF_P4T = OFF_WTOK + NM * E
    OFF_CONST = OFF_P4T + 64
    C_S = OFF_CONST
    C_NS = C_S + 16
    C_BD = C_NS + 16
    C_QD = C_BD + 16
    C_KD = C_QD + 32
    C_IOTP = C_KD + 32
    C_BIAS = C_IOTP + 8
    C_IDX0 = C_BIAS + 8
    C_IDX1 = C_IDX0 + 256
    C_RIDX0 = C_IDX1 + 256
    C_RIDX1 = C_RIDX0 + 256
    C_MASK0 = C_RIDX1 + 256
    C_MASK1 = C_MASK0 + 256
    C_IDENT = C_MASK1 + 256
    OFF_REST = ceil128(C_IDENT + 128)
    R = OFF_REST
    OFF_XNTP = R
    P0_XT0 = R + 16384 + 6144
    P0_XN = P0_XT0 + DM
    P0_ST = P0_XN + DM
    P0_END = P0_ST + 128
    P1_WK = R + 16384
    P1_WV = P1_WK + 2048
    P1_KKD = P1_WV + 2048
    P1_VST = P1_KKD + 1024
    P1_END = P1_VST + 1024
    P2_W0 = R
    P2_W1 = P2_W0 + 2048
    P2_EW0 = R
    P2_EW1 = R + 4096
    P2_QT = R + 4096
    P2_KT = P2_QT + 1024
    P2_KTOK = P2_KT + 1024
    P2_VTOK = P2_KTOK + 1024
    P2_VT = P2_VTOK + 1024
    P2_D = P2_VT
    P2_QKD = P2_VT + 512
    P2_KKDS = P2_VT + 1024
    P2_O = P2_KKDS + 256
    P2_OUT = P2_O + 16384
    P2_RMS = P2_OUT + 2048
    P2_SSQ = P2_RMS + 8
    P2_ST = P2_SSQ + 8
    P2_SB = P2_ST + 16
    P2_SQS = P2_SB
    P2_END = ceil128(P2_SB + 512)
    P35 = R + 16384
    T_WR = P35
    P3_LG = T_WR + 512
    P3_SC = P3_LG + 64
    P3_M8 = P3_SC + 64
    P3_ST = P3_M8 + 16
    T_WT = ceil128(P3_ST + 16)
    T_MASK = T_WT + 1024
    T_CUMA = T_MASK + 1024
    T_CUMB = T_CUMA + 1024
    T_POSM = T_CUMB + 1024
    T_CNT = T_POSM + 1024
    T_POSTOK = T_CNT + 128
    T_M8B = T_POSTOK + 256
    T_ONE = T_M8B + 8
    P3_END = T_ONE + 64
    P4_GU0 = R
    P4_GU1 = P4_GU0 + 4096
    P4_DW0 = P4_GU1 + 4096
    P4_DW1 = P4_DW0 + DM
    P4_H = P4_DW1 + DM
    P4_GS = P4_H + 512
    P4_YAC = P4_GS + 512
    P4_XTI = P4_YAC + 8192
    P4_STI = P4_XTI + DM
    P4_END = P4_STI + DM
    S4_XS = P4_DW1 + DM
    S4_XST = S4_XS + 4096
    S4_HT = S4_XST + 6144
    S4_GS = S4_HT + 2304
    S4_YT = S4_GS + 192
    S4_END = S4_YT + 2048
    S4_GB0 = S4_XS
    S4_GB1 = S4_GB0 + 2048
    S4_CB = S4_GB1 + 2048
    S4_CX = S4_CB + 2048
    S4_CS = S4_CX + 2048
    P4_END = max(P4_END, S4_END, S4_CS + 2048)

    ARENA = ceil128(max(P0_END, P1_END, P2_END, P3_END, P4_END))
    assert ARENA <= 49152, f"arena {ARENA} cols exceeds SBUF"

    r = Rec()

    with (
        nc.sbuf_tensor("A", [128, ARENA], F32) as A,
        nc.sbuf_tensor("IDXT", [128, 64], mybir.dt.int32) as IDXT,
        nc.psum_tensor("pb0", [128, 512], F32) as pb0,
        nc.psum_tensor("pb1", [128, 512], F32) as pb1,
        nc.psum_tensor("pb2", [128, 512], F32) as pb2,
        nc.psum_tensor("pb3", [128, 512], F32) as pb3,
        nc.psum_tensor("pb4", [128, 512], F32) as pb4,
        nc.psum_tensor("pb5", [128, 512], F32) as pb5,
        nc.psum_tensor("pb6", [128, 512], F32) as pb6,
        nc.psum_tensor("pb7", [128, 512], F32) as pb7,
        nc.semaphore("dW") as sem_dW,
        nc.semaphore("dX") as sem_dX,
        nc.semaphore("tM") as sem_tM,
        nc.semaphore("vV") as sem_vV,
        nc.semaphore("aA") as sem_aA,
        nc.semaphore("gG") as sem_gG,
        nc.Block() as block,
    ):
        PB = [pb0, pb1, pb2, pb3, pb4, pb5, pb6, pb7]
        SEMS = {"dW": sem_dW, "dX": sem_dX, "tM": sem_tM, "vV": sem_vV,
                "aA": sem_aA, "gG": sem_gG}

        def sl(off, n):
            return A[:, off:off + n]

        bank_free = [None] * 8
        shared = {}

        def bank_wait(eng, b):
            if bank_free[b] is not None:
                r.wait(eng, *bank_free[b])

        def wchunk_src(dram, col0, ncols):
            return (dram[:, col0:col0 + ncols]
                    .rearrange("(k p) n -> k p n", p=128).transpose([1, 0, 2]))

        def emit_consts():
            r.dW(lambda e: e.dma_start(out=sl(C_S, 16),
                                       in_=slope_d[None, :].broadcast_to([128, H])))
            r.wait("vector", "dW")
            r.v(lambda e: e.tensor_scalar_mul(sl(C_NS, 16), sl(C_S, 16), -1.0))
            r.wait("scalar", "vV")
            r.a(lambda e: e.activation(sl(C_BD, 16), sl(C_S, 16), AF.Exp, scale=-float(BLK)))
            r.g(lambda e: e.iota(sl(C_IOTP, 1), pattern=[[0, 1]], base=0,
                                 channel_multiplier=1, allow_small_or_imprecise_dtypes=True))
            r.g(lambda e: e.iota(sl(C_IDX0, 256), pattern=[[1, 256]], base=0,
                                 channel_multiplier=-1, allow_small_or_imprecise_dtypes=True))
            r.g(lambda e: e.iota(sl(C_IDX1, 256), pattern=[[1, 256]], base=-128,
                                 channel_multiplier=-1, allow_small_or_imprecise_dtypes=True))
            r.v(lambda e: e.memset(sl(C_MASK0, 512 + 128), 1.0))
            r.wait("gpsimd", "vV")
            r.g(lambda e: e.affine_select(sl(C_MASK0, 256), sl(C_MASK0, 256),
                                          pattern=[[1, 256]], base=0, channel_multiplier=-1,
                                          compare_op=AL.is_ge, fill=0.0))
            r.g(lambda e: e.affine_select(sl(C_MASK1, 256), sl(C_MASK1, 256),
                                          pattern=[[1, 256]], base=-128, channel_multiplier=-1,
                                          compare_op=AL.is_ge, fill=0.0))
            r.g(lambda e: e.affine_select(sl(C_IDENT, 128), sl(C_IDENT, 128),
                                          pattern=[[-1, 128]], base=0, channel_multiplier=1,
                                          compare_op=AL.is_equal, fill=0.0))
            r.wait("vector", "gG")
            r.v(lambda e: e.tensor_scalar_max(sl(C_RIDX0, 512), sl(C_IDX0, 512), 0.0))
            for h in range(H):
                for c in range(2):
                    r.wait("vector", "aA")
                    r.wait("vector", "gG")
                    r.v(lambda e, _h=h, _c=c: e.tensor_scalar_mul(
                        sl(C_BIAS, 1), sl(C_NS + _h, 1), float(128 * _c + 1)))
                    r.v(lambda e, _h=h, _c=c: e.tensor_scalar_mul(
                        sl(C_BIAS + 1, 1), sl(C_S + _h, 1), float(128 * _c - 255)))
                    r.wait("scalar", "vV")
                    r.wait("scalar", "gG")
                    r.a(lambda e, _h=h, _c=c: e.activation(
                        sl(C_QD + 2 * _h + _c, 1), sl(C_IOTP, 1), AF.Exp,
                        scale=sl(C_NS + _h, 1), bias=sl(C_BIAS, 1)))
                    r.a(lambda e, _h=h, _c=c: e.activation(
                        sl(C_KD + 2 * _h + _c, 1), sl(C_IOTP, 1), AF.Exp,
                        scale=sl(C_S + _h, 1), bias=sl(C_BIAS + 1, 1)))
            r.v(lambda e: e.memset(sl(OFF_KV, SZ_KV), 0.0))
            if SPARSE:
                zt = r.v(lambda e: e.memset(sl(P0_XN, DM), 0.0))
                r.wait("sync", "vV", zt)
                for i in range(E * CAP // 128):
                    r.dW(lambda e, _i=i: e.dma_start(
                        out=xs_d[_i * 128:(_i + 1) * 128, :], in_=sl(P0_XN, DM)))

        ln_state = {"buf_release": 0, "xn_spill": 0}

        def emit_ln_tile(src_d, xnt_off, m, spill, bank_base):
            st = sl(P0_ST, 8)
            if ln_state["buf_release"]:
                r.wait("gpsimd", "vV", ln_state["buf_release"])
            ld = r.dX(lambda e, _m=m: e.dma_start(
                out=sl(P0_XT0, DM), in_=src_d[_m * 128:(_m + 1) * 128, :]))
            r.wait("vector", "dX", ld)
            r.v(lambda e: e.reduce_sum(st[:, 0:1], sl(P0_XT0, DM), axis=AX.X))
            r.v(lambda e: e.tensor_scalar_mul(st[:, 0:1], st[:, 0:1], 1.0 / DM))
            r.wait("scalar", "dX", max(ld, ln_state["xn_spill"]))
            r.wait("scalar", "tM")
            r.a(lambda e: e.activation(sl(P0_XN, DM), sl(P0_XT0, DM), AF.Square))
            r.wait("vector", "aA")
            r.v(lambda e: e.reduce_sum(st[:, 1:2], sl(P0_XN, DM), axis=AX.X))
            r.v(lambda e: e.tensor_scalar(st[:, 1:2], st[:, 1:2], 1.0 / DM, LN_EPS,
                                          op0=AL.mult, op1=AL.add))
            r.v(lambda e: e.tensor_mul(st[:, 2:3], st[:, 0:1], st[:, 0:1]))
            r.v(lambda e: e.tensor_sub(st[:, 1:2], st[:, 1:2], st[:, 2:3]))
            r.wait("scalar", "vV")
            r.a(lambda e: e.activation(st[:, 3:4], st[:, 1:2], AF.Sqrt))
            r.wait("vector", "aA")
            r.v(lambda e: e.reciprocal(st[:, 4:5], st[:, 3:4]))
            r.v(lambda e: e.tensor_mul(st[:, 5:6], st[:, 0:1], st[:, 4:5]))
            r.v(lambda e: e.tensor_scalar(sl(P0_XN, DM), sl(P0_XT0, DM),
                                          st[:, 4:5], st[:, 5:6],
                                          op0=AL.mult, op1=AL.subtract))
            xn_done = r.cnt["vV"]
            ln_state["buf_release"] = xn_done
            if spill:
                r.wait("gpsimd", "vV", xn_done)
                ln_state["xn_spill"] = r.dX(lambda e, _m=m: e.dma_start(
                    out=xn_d[_m * 128:(_m + 1) * 128, :], in_=sl(P0_XN, DM)))
            r.wait("tensor", "vV", xn_done)
            r.wait("tensor", "gG")
            for grp in range(4):
                b = bank_base + grp % 2
                bank_wait("tensor", b)
                for f4 in range(4):
                    f = grp * 4 + f4
                    r.t(lambda e, _f=f, _f4=f4, _b=b: e.transpose(
                        PB[_b][:, _f4 * 128:(_f4 + 1) * 128],
                        sl(P0_XN + _f * 128, 128), sl(C_IDENT, 128)),
                        inc=(f4 == 3))
                tdone = r.cnt["tM"]
                r.wait("vector", "tM", tdone)
                d0 = xnt_off + (grp * 4) * TPC + m * 128
                ev = r.v(lambda e, _d=d0, _b=b: e.tensor_copy(
                    A[:, _d:_d + 4 * TPC].rearrange("p (k n) -> p k n", n=TPC)[:, :, 0:128],
                    PB[_b].rearrange("p (k n) -> p k n", n=128)))
                bank_free[b] = ("vV", ev)

        p1_state = {"w_release": 0}

        def emit_pre_head(h):
            if p1_state["w_release"]:
                r.wait("sync", "tM", p1_state["w_release"])
            r.dW(lambda e, _h=h: e.dma_start(
                out=sl(P1_WK, 2048).rearrange("p (k n) -> p k n", n=128),
                in_=wchunk_src(wqkvuT_d, DM + _h * DH, 128)))
            r.dW(lambda e, _h=h: e.dma_start(
                out=sl(P1_WV, 2048).rearrange("p (k n) -> p k n", n=128),
                in_=wchunk_src(wqkvuT_d, 2 * DM + _h * DH, 128)))
            w_done = r.cnt["dW"]
            r.wait("tensor", "dW", w_done)
            for (woff, store, is_k) in ((P1_WK, P1_KKD, True), (P1_WV, P1_VST, False)):
                for m in range(NM):
                    b = (0 if is_k else 2) + m % 2
                    bank_wait("tensor", b)
                    for k in range(16):
                        r.t(lambda e, _k=k, _m=m, _w=woff, _b=b: e.matmul(
                            PB[_b][:, 0:128],
                            sl(OFF_XNTP + _k * TPC + _m * 128, 128),
                            sl(_w + _k * 128, 128),
                            start=(_k == 0), stop=(_k == 15)), inc=(k == 15))
                    tdone = r.cnt["tM"]
                    if is_k:
                        r.wait("scalar", "tM", tdone)
                        sdone = r.a(lambda e, _m=m, _b=b: e.activation(
                            sl(P1_KKD + _m * 128, 128), PB[_b][:, 0:128], AF.Silu))
                        bank_free[b] = ("aA", sdone)
                        r.wait("vector", "aA", sdone)
                        r.v(lambda e, _m=m, _h=h: e.tensor_scalar(
                            sl(P1_KKD + _m * 128, 128), sl(P1_KKD + _m * 128, 128),
                            sl(C_KD + 2 * _h + (_m % 2), 1), None, op0=AL.mult))
                    else:
                        r.wait("vector", "tM", tdone)
                        vdone = r.v(lambda e, _m=m, _b=b: e.tensor_copy(
                            sl(P1_VST + _m * 128, 128), PB[_b][:, 0:128]))
                        bank_free[b] = ("vV", vdone)
            p1_state["w_release"] = r.cnt["tM"]
            proj_done = r.cnt["vV"]
            r.wait("tensor", "vV", proj_done)
            for blk in range(NBLK):
                b = 4 + blk % 2
                bank_wait("tensor", b)
                for mc in range(2):
                    m = blk * 2 + mc
                    r.t(lambda e, _m=m, _b=b: e.matmul(
                        PB[_b][:, 0:128],
                        sl(P1_KKD + _m * 128, 128),
                        sl(P1_VST + _m * 128, 128),
                        start=(_m % 2 == 0), stop=(_m % 2 == 1)), inc=(mc == 1))
                tdone = r.cnt["tM"]
                r.wait("vector", "tM", tdone)
                vdone = r.v(lambda e, _h=h, _b=b: e.scalar_tensor_tensor(
                    sl(OFF_KV + _h * 128, 128), sl(OFF_KV + _h * 128, 128),
                    sl(C_BD + _h, 1), PB[_b][:, 0:128], op0=AL.mult, op1=AL.add))
                bank_free[b] = ("vV", vdone)


        slot_free = [None, None]

        def load_w_chunk(slot_idx, dram, col0, ncols):
            woff = (P2_W0, P2_W1)[slot_idx]
            if slot_free[slot_idx] is not None:
                r.wait("sync", *slot_free[slot_idx])
            cnt = r.dW(lambda e, _w=woff, _d=dram, _c=col0, _n=ncols: e.dma_start(
                out=sl(_w, 16 * _n).rearrange("p (k n) -> p k n", n=_n),
                in_=wchunk_src(_d, _c, _n)))
            return woff, cnt

        def emit_attention():
            if SPARSE:
                r.wait("gpsimd", "vV", shared["cast_done"])
                for m in range(NM):
                    r.dX(lambda e, _m=m: e.dma_start(
                        out=sl(P2_OUT, DM), in_=xn_d[_m * 128:(_m + 1) * 128, :]))
                    for j in range(4):
                        r.dX(lambda e, _m=m, _j=j: e.indirect_dma_start(
                            out=xs_d,
                            out_offset=bass.IndirectOffsetOnAxis(
                                IDXT[:, _m * 4 + _j:_m * 4 + _j + 1], 0),
                            in_=sl(P2_OUT, DM), in_offset=None))
                shared["scat_end"] = r.cnt["dX"]
            for h in range(H):
                specs = ((h * DH, P2_QT, 0), (DM + h * DH, P2_KT, 2),
                         (2 * DM + h * DH, P2_VT, 4))
                for si, (col0, dst, pbase) in enumerate(specs):
                    wq, cq = load_w_chunk(si % 2, wqkvuT_d, col0, DH)
                    r.wait("tensor", "dW", cq)
                    bank_wait("tensor", pbase)
                    bank_wait("tensor", pbase + 1)
                    for k in range(16):
                        for tc in range(2):
                            r.t(lambda e, _k=k, _tc=tc, _w=wq, _pb=pbase: e.matmul(
                                PB[_pb + _tc][:, 0:512],
                                sl(_w + _k * DH, DH),
                                sl(OFF_XNT + _k * TPC + _tc * 512, 512),
                                start=(_k == 0), stop=(_k == 15)),
                                inc=(k == 15 and tc == 1))
                    pj_done = r.cnt["tM"]
                    slot_free[si % 2] = ("tM", pj_done)
                    if si < 2:
                        r.wait("scalar", "tM", pj_done)
                        ev = None
                        for tc in range(2):
                            ev = r.a(lambda e, _tc=tc, _d=dst, _pb=pbase: e.activation(
                                sl(_d + _tc * 512, 512), PB[_pb + _tc][:, 0:512], AF.Silu))
                        bank_free[pbase] = ("aA", ev)
                        bank_free[pbase + 1] = ("aA", ev)
                    else:
                        r.wait("vector", "tM", pj_done)
                        ev = None
                        for tc in range(2):
                            ev = r.v(lambda e, _tc=tc, _d=dst, _pb=pbase: e.tensor_copy(
                                sl(_d + _tc * 512, 512), PB[_pb + _tc][:, 0:512]))
                        bank_free[pbase] = ("vV", ev)
                        bank_free[pbase + 1] = ("vV", ev)
                r.wait("tensor", "aA")
                r.wait("tensor", "vV")
                for (srcoff, dstoff) in ((P2_KT, P2_KTOK), (P2_VT, P2_VTOK)):
                    for grp in range(2):
                        b = 6 + grp % 2
                        bank_wait("tensor", b)
                        for q4 in range(4):
                            m = grp * 4 + q4
                            r.t(lambda e, _m=m, _q=q4, _b=b, _s=srcoff: e.transpose(
                                PB[_b][:, _q * 128:(_q + 1) * 128],
                                sl(_s + _m * 128, 128), sl(C_IDENT, 128)),
                                inc=(q4 == 3))
                        td = r.cnt["tM"]
                        r.wait("vector", "tM", td)
                        ev = r.v(lambda e, _d=dstoff, _g=grp, _b=b: e.tensor_copy(
                            sl(_d + _g * 512, 512), PB[_b][:, 0:512]))
                        bank_free[b] = ("vV", ev)
                r.wait("scalar", "vV")
                for c in range(2):
                    r.a(lambda e, _c=c, _h=h: e.activation(
                        sl(P2_D + _c * 256, 256), sl(C_RIDX0 + _c * 256, 256),
                        AF.Exp, scale=sl(C_NS + _h, 1)))
                dd = r.cnt["aA"]
                r.wait("vector", "aA", dd)
                r.v(lambda e: e.tensor_mul(sl(P2_D, 512), sl(P2_D, 512),
                                           sl(C_MASK0, 512)))
                for mt in range(NM):
                    r.v(lambda e, _mt=mt, _h=h: e.tensor_scalar(
                        sl(P2_KTOK + _mt * 128, 128), sl(P2_KTOK + _mt * 128, 128),
                        sl(C_KD + 2 * _h + (_mt % 2), 1), None, op0=AL.mult))
                dk_done = r.cnt["vV"]
                for blk in range(NBLK):
                    t0 = blk * BLK
                    r.wait("tensor", "vV")
                    bank_wait("tensor", 1)
                    for sc in range(2):
                        r.t(lambda e, _sc=sc, _t=t0: e.matmul(
                            pb1[:, _sc * 256:(_sc + 1) * 256],
                            sl(P2_KT + _t + _sc * 128, 128),
                            sl(P2_QT + _t, 256),
                            start=True, stop=True), inc=(sc == 1))
                    qk_done = r.cnt["tM"]
                    r.wait("vector", "tM", qk_done)
                    qkd = r.v(lambda e: e.tensor_mul(sl(P2_QKD, 512), pb1[:, 0:512],
                                                     sl(P2_D, 512)))
                    bank_free[1] = ("vV", qkd)
                    bank_wait("tensor", 0)
                    for mc in range(2):
                        r.t(lambda e, _mc=mc, _h=h, _t=t0: e.matmul(
                            pb0[:, _mc * 128:(_mc + 1) * 128],
                            sl(P2_QT + _t + _mc * 128, 128),
                            sl(OFF_KV + _h * 128, 128),
                            start=True, stop=True), inc=(mc == 1))
                    bank_wait("tensor", 3)
                    for sc in range(2):
                        r.t(lambda e, _sc=sc, _b=blk: e.matmul(
                            pb3[:, 0:128],
                            sl(P2_KTOK + (_b * 2 + _sc) * 128, 128),
                            sl(P2_VTOK + (_b * 2 + _sc) * 128, 128),
                            start=(_sc == 0), stop=(_sc == 1)), inc=(sc == 1))
                    psC_done = r.cnt["tM"]
                    r.wait("tensor", "vV", qkd)
                    bank_wait("tensor", 2)
                    for mc in range(2):
                        for sc in range(2):
                            r.t(lambda e, _mc=mc, _sc=sc, _b=blk: e.matmul(
                                pb2[:, _mc * 128:(_mc + 1) * 128],
                                sl(P2_QKD + _sc * 256 + _mc * 128, 128),
                                sl(P2_VTOK + (_b * 2 + _sc) * 128, 128),
                                start=(_sc == 0), stop=(_sc == 1)),
                                inc=(mc == 1 and sc == 1))
                    psB_done = r.cnt["tM"]
                    r.wait("vector", "tM", psB_done)
                    ev = None
                    for mc in range(2):
                        m = blk * 2 + mc
                        dst = P2_O + m * 2048 + h * 128
                        r.v(lambda e, _d=dst, _mc=mc, _h=h: e.tensor_scalar(
                            sl(_d, 128), pb0[:, _mc * 128:(_mc + 1) * 128],
                            sl(C_QD + 2 * _h + _mc, 1), None, op0=AL.mult))
                        ev = r.v(lambda e, _d=dst, _mc=mc: e.tensor_add(
                            sl(_d, 128), sl(_d, 128), pb2[:, _mc * 128:(_mc + 1) * 128]))
                    bank_free[0] = ("vV", ev)
                    bank_free[2] = ("vV", ev)
                    r.wait("vector", "tM", psC_done)
                    kvu = r.v(lambda e, _h=h: e.scalar_tensor_tensor(
                        sl(OFF_KV + _h * 128, 128), sl(OFF_KV + _h * 128, 128),
                        sl(C_BD + _h, 1), pb3[:, 0:128], op0=AL.mult, op1=AL.add))
                    bank_free[3] = ("vV", kvu)
            st = sl(P2_ST, 16)
            for m in range(NM):
                for c in range(4):
                    r.v(lambda e, _m=m, _c=c: e.tensor_mul(
                        sl(P2_SB, 512), sl(P2_O + _m * 2048 + _c * 512, 512),
                        sl(P2_O + _m * 2048 + _c * 512, 512)))
                    r.v(lambda e, _c=c: e.reduce_sum(st[:, 3:4], sl(P2_SB, 512), axis=AX.X))
                    if c == 0:
                        r.v(lambda e, _m=m: e.tensor_copy(sl(P2_SSQ + _m, 1), st[:, 3:4]))
                    else:
                        r.v(lambda e, _m=m: e.tensor_add(
                            sl(P2_SSQ + _m, 1), sl(P2_SSQ + _m, 1), st[:, 3:4]))
                r.v(lambda e, _m=m: e.tensor_scalar(
                    st[:, 1:2], sl(P2_SSQ + _m, 1), 1.0 / DM, RMS_EPS,
                    op0=AL.mult, op1=AL.add))
                r.wait("scalar", "vV")
                r.a(lambda e: e.activation(st[:, 2:3], st[:, 1:2], AF.Sqrt))
                r.wait("vector", "aA")
                r.v(lambda e, _m=m: e.reciprocal(sl(P2_RMS + _m, 1), st[:, 2:3]))
            ew_free = [("tM", r.cnt["tM"]), ("tM", r.cnt["tM"])]
            sb_spill = [0, 0]

            def load_ew(slot, dram, col0):
                woff = (P2_EW0, P2_EW1)[slot]
                if ew_free[slot] is not None:
                    r.wait("sync", *ew_free[slot])
                cnt = r.dW(lambda e, _w=woff, _d=dram, _c=col0: e.dma_start(
                    out=sl(_w, 4096).rearrange("p (k n) -> p k n", n=256),
                    in_=wchunk_src(_d, _c, 256)))
                return woff, cnt

            for uc in range(8):
                wu, cu = load_ew(uc % 2, wqkvuT_d, 3 * DM + uc * 256)
                r.wait("tensor", "dW", cu)
                for m in range(NM):
                    b = uc % 2
                    bank_wait("tensor", b)
                    for k in range(16):
                        r.t(lambda e, _k=k, _m=m, _w=wu, _b=b: e.matmul(
                            PB[_b][:, 0:256],
                            sl(OFF_XNT + _k * TPC + _m * 128, 128),
                            sl(_w + _k * 256, 256),
                            start=(_k == 0), stop=(_k == 15)), inc=(k == 15))
                    ud = r.cnt["tM"]
                    r.wait("vector", "tM", ud)
                    dst = P2_O + m * 2048 + uc * 256
                    ev = r.v(lambda e, _d=dst, _b=b: e.tensor_mul(
                        sl(_d, 256), sl(_d, 256), PB[_b][:, 0:256]))
                    bank_free[b] = ("vV", ev)
                ew_free[uc % 2] = ("tM", r.cnt["tM"])
            if SPARSE:
                r.wait("vector", "dX", shared["scat_end"])
            for m in range(NM):
                ou_done = r.cnt["vV"]
                r.wait("tensor", "vV", ou_done)
                for grp in range(4):
                    b = 6 + grp % 2
                    bank_wait("tensor", b)
                    for f4 in range(4):
                        f = grp * 4 + f4
                        r.t(lambda e, _f=f, _f4=f4, _m=m, _b=b: e.transpose(
                            PB[_b][:, _f4 * 128:(_f4 + 1) * 128],
                            sl(P2_O + _m * 2048 + _f * 128, 128), sl(C_IDENT, 128)),
                            inc=(f4 == 3))
                    tdone = r.cnt["tM"]
                    r.wait("vector", "tM", tdone)
                    ev = r.v(lambda e, _g=grp, _b=b: e.tensor_copy(
                        sl(P2_OUT + _g * 512, 512), PB[_b][:, 0:512]))
                    bank_free[b] = ("vV", ev)
                out_done = r.cnt["vV"]
                for ch in range(8):
                    ww, cw = load_ew(ch % 2, woutT_d, ch * 256)
                    r.wait("tensor", "dW", cw)
                    r.wait("tensor", "vV", out_done)
                    b = 4 + ch % 2
                    bank_wait("tensor", b)
                    for k in range(16):
                        r.t(lambda e, _k=k, _w=ww, _b=b: e.matmul(
                            PB[_b][:, 0:256],
                            sl(P2_OUT + _k * 128, 128),
                            sl(_w + _k * 256, 256),
                            start=(_k == 0), stop=(_k == 15)), inc=(k == 15))
                    s_done = r.cnt["tM"]
                    ew_free[ch % 2] = ("tM", s_done)
                    r.wait("vector", "tM", s_done)
                    sb = P2_SB + (ch % 2) * 256
                    if sb_spill[ch % 2]:
                        r.wait("vector", "dX", sb_spill[ch % 2])
                    ev = r.v(lambda e, _b=b, _m=m, _s=sb: e.tensor_scalar(
                        sl(_s, 256), PB[_b][:, 0:256], sl(P2_RMS + _m, 1), None,
                        op0=AL.mult))
                    bank_free[b] = ("vV", ev)
                    r.wait("gpsimd", "vV", ev)
                    sb_spill[ch % 2] = r.dX(lambda e, _r=m * 128, _c=ch, _s=sb: e.dma_start(
                        out=s_spill_d[_r:_r + 128, _c * 256:(_c + 1) * 256],
                        in_=sl(_s, 256)))


        def emit_router():
            woff = T_WR
            r.dW(lambda e, _w=woff: e.dma_start(
                out=sl(_w, 16 * E).rearrange("p (k n) -> p k n", n=E),
                in_=wrT_d.rearrange("(k p) n -> k p n", p=128).transpose([1, 0, 2])))
            wr_done = r.cnt["dW"]
            r.wait("tensor", "dW", wr_done)
            for m in range(NM):
                b = 6 + m % 2
                bank_wait("tensor", b)
                for k in range(16):
                    r.t(lambda e, _k=k, _m=m, _w=woff, _b=b: e.matmul(
                        PB[_b][:, 0:E],
                        sl(OFF_XNT + _k * TPC + _m * 128, 128),
                        sl(_w + _k * E, E),
                        start=(_k == 0), stop=(_k == 15)), inc=(k == 15))
                lg_done = r.cnt["tM"]
                st = sl(P3_ST, 16)
                r.wait("vector", "tM", lg_done)
                r.wait("vector", "aA")
                r.v(lambda e, _b=b: e.reduce_max(st[:, 0:1], PB[_b][:, 0:E],
                                                 axis=AX.X, negate=True))
                r.wait("scalar", "vV")
                r.a(lambda e, _b=b: e.activation(sl(P3_LG, E), PB[_b][:, 0:E],
                                                 AF.Exp, bias=st[:, 0:1]))
                exp_done = r.cnt["aA"]
                bank_free[b] = ("aA", exp_done)
                r.wait("vector", "aA", exp_done)
                r.v(lambda e: e.reduce_sum(st[:, 1:2], sl(P3_LG, E), axis=AX.X))
                r.v(lambda e: e.reciprocal(st[:, 2:3], st[:, 1:2]))
                r.v(lambda e: e.tensor_scalar(sl(P3_LG, E), sl(P3_LG, E),
                                              st[:, 2:3], None, op0=AL.mult))
                r.v(lambda e: e.max(out=sl(P3_M8, 8), in_=sl(P3_LG, E)))
                r.v(lambda e: e.memset(sl(P3_M8 + 4, 4), -1.0))
                r.v(lambda e: e.match_replace(out=sl(P3_SC, E), in_to_replace=sl(P3_M8, 8),
                                              in_values=sl(P3_LG, E), imm_value=0.0))
                r.v(lambda e: e.tensor_sub(sl(P3_SC, E), sl(P3_LG, E), sl(P3_SC, E)))
                r.v(lambda e: e.reduce_sum(st[:, 3:4], sl(P3_SC, E), axis=AX.X))
                r.v(lambda e: e.reciprocal(st[:, 4:5], st[:, 3:4]))
                r.v(lambda e, _m=m: e.tensor_scalar(sl(OFF_WTOK + _m * E, E), sl(P3_SC, E),
                                                    st[:, 4:5], None, op0=AL.mult))

        def emit_tables_scatter():
            r.wait("tensor", "vV")
            for m in range(NM):
                b = 6 + m % 2
                bank_wait("tensor", b)
                r.t(lambda e, _m=m, _b=b: e.transpose(
                    PB[_b][0:32, 0:128], sl(OFF_WTOK + _m * E, E), sl(C_IDENT, 128)),
                    inc=True)
                td = r.cnt["tM"]
                r.wait("vector", "tM", td)
                ev = r.v(lambda e, _m=m, _b=b: e.tensor_copy(
                    A[0:32, T_WT + _m * 128:T_WT + (_m + 1) * 128], PB[_b][0:32, 0:128]))
                bank_free[b] = ("vV", ev)
            WT = A[0:32, T_WT:T_WT + TPC]
            MASK = A[0:32, T_MASK:T_MASK + TPC]
            r.v(lambda e: e.tensor_scalar(MASK, WT, 0.0, None, op0=AL.is_gt))
            r.v(lambda e: e.reduce_sum(A[0:32, T_CNT:T_CNT + 1], MASK, axis=AX.X))
            cur, nxt = T_CUMA, T_CUMB
            r.v(lambda e: e.tensor_copy(A[0:32, T_CUMA:T_CUMA + TPC], MASK))
            s = 1
            while s < TPC:
                r.v(lambda e, _c=cur, _n=nxt, _s=s: e.tensor_copy(
                    A[0:32, _n:_n + _s], A[0:32, _c:_c + _s]))
                r.v(lambda e, _c=cur, _n=nxt, _s=s: e.tensor_add(
                    A[0:32, _n + _s:_n + TPC], A[0:32, _c + _s:_c + TPC],
                    A[0:32, _c:_c + TPC - _s]))
                cur, nxt = nxt, cur
                s *= 2
            r.v(lambda e, _c=cur: e.tensor_scalar(
                A[0:32, T_POSM:T_POSM + TPC], A[0:32, _c:_c + TPC],
                A[0:32, T_CNT:T_CNT + 1], None, op0=AL.subtract))
            r.v(lambda e: e.tensor_scalar_mul(A[0:32, T_POSM:T_POSM + TPC],
                                              A[0:32, T_POSM:T_POSM + TPC], -1.0))
            r.v(lambda e: e.tensor_scalar_min(A[0:32, T_POSM:T_POSM + TPC],
                                              A[0:32, T_POSM:T_POSM + TPC], float(CAP - 1)))
            r.v(lambda e: e.tensor_scalar_mul(A[0:32, T_CNT + 1:T_CNT + 2],
                                              A[0:32, C_IOTP:C_IOTP + 1], float(CAP)))
            r.v(lambda e: e.tensor_scalar(A[0:32, T_POSM:T_POSM + TPC],
                                          A[0:32, T_POSM:T_POSM + TPC],
                                          A[0:32, T_CNT + 1:T_CNT + 2], 1.0,
                                          op0=AL.add, op1=AL.add))
            r.v(lambda e: e.tensor_mul(A[0:32, T_POSM:T_POSM + TPC],
                                       A[0:32, T_POSM:T_POSM + TPC], MASK))
            pm_done = r.cnt["vV"]
            r.wait("tensor", "vV", pm_done)
            for m in range(NM):
                b = 6 + m % 2
                bank_wait("tensor", b)
                r.t(lambda e, _m=m, _b=b: e.transpose(
                    PB[_b][:, 0:32], A[0:32, T_POSM + _m * 128:T_POSM + (_m + 1) * 128],
                    A[0:32, C_IDENT:C_IDENT + 32]), inc=True)
                td = r.cnt["tM"]
                r.wait("vector", "tM", td)
                ev = r.v(lambda e, _m=m, _b=b: e.tensor_copy(
                    sl(T_POSTOK + _m * E, E), PB[_b][:, 0:32]))
                bank_free[b] = ("vV", ev)
                r.v(lambda e, _m=m: e.max(out=sl(T_M8B, 8), in_=sl(T_POSTOK + _m * E, E)))
                r.v(lambda e, _m=m: e.tensor_scalar(
                    sl(OFF_P4T + _m * 4, 4), sl(T_M8B, 4), -1.0, None, op0=AL.add))
                for j in range(4):
                    r.v(lambda e, _m=m, _j=j: e.tensor_scalar(
                        sl(T_ONE, E), sl(T_POSTOK + _m * E, E),
                        sl(T_M8B + _j, 1), None, op0=AL.is_equal))
                    r.v(lambda e, _m=m: e.tensor_mul(
                        sl(T_ONE, E), sl(T_ONE, E), sl(OFF_WTOK + _m * E, E)))
                    r.v(lambda e, _m=m, _j=j: e.reduce_sum(
                        sl(OFF_P4T + 32 + _m * 4 + _j, 1), sl(T_ONE, E), axis=AX.X))
            r.v(lambda e: e.tensor_copy(IDXT[:, 0:32], sl(OFF_P4T, 32)))
            shared["cast_done"] = r.cnt["vV"]

        def emit_ffn_sparse():
            gu_free = [None, None]
            dw_free = [None, None]
            xs_state = {}

            def load_xs(e_idx):
                r.wait("sync", "dX", shared["scat_end"])
                r.dW(lambda e, _e=e_idx: e.dma_start(
                    out=sl(S4_XS, DM),
                    in_=xs_d[_e * CAP:_e * CAP + 128, :]))
                xs_state[e_idx] = r.dW(lambda e, _e=e_idx: e.dma_start(
                    out=A[0:64, S4_XS + DM:S4_XS + 2 * DM],
                    in_=xs_d[_e * CAP + 128:(_e + 1) * CAP, :]))

            def transposes(ei):
                xb = S4_XST + (ei % 2) * 3072
                r.wait("tensor", "dW", xs_state[ei])
                for grp in range(4):
                    b = 4 + grp % 2
                    bank_wait("tensor", b)
                    for q in range(4):
                        f = grp * 4 + q
                        r.t(lambda e, _f=f, _q=q, _b=b: e.transpose(
                            PB[_b][:, _q * 128:(_q + 1) * 128],
                            sl(S4_XS + _f * 128, 128), sl(C_IDENT, 128)),
                            inc=(q == 3))
                    td = r.cnt["tM"]
                    r.wait("vector", "tM", td)
                    ev = r.v(lambda e, _g=grp, _b=b, _x=xb: e.tensor_copy(
                        A[:, _x + _g * 4 * 192:_x + (_g * 4 + 4) * 192]
                        .rearrange("p (k n) -> p k n", n=192)[:, :, 0:128],
                        PB[_b].rearrange("p (k n) -> p k n", n=128)))
                    bank_free[b] = ("vV", ev)
                for grp in range(2):
                    b = 4 + grp % 2
                    bank_wait("tensor", b)
                    for q in range(8):
                        f = grp * 8 + q
                        r.t(lambda e, _f=f, _q=q, _b=b: e.transpose(
                            PB[_b][:, _q * 64:(_q + 1) * 64],
                            A[0:64, S4_XS + DM + _f * 128:S4_XS + DM + (_f + 1) * 128],
                            A[0:64, C_IDENT:C_IDENT + 64]),
                            inc=(q == 7))
                    td = r.cnt["tM"]
                    r.wait("vector", "tM", td)
                    ev = r.v(lambda e, _g=grp, _b=b, _x=xb: e.tensor_copy(
                        A[:, _x + 128 + _g * 8 * 192:_x + 128 + (_g * 8 + 8) * 192]
                        .rearrange("p (k n) -> p k n", n=192)[:, :, 0:64],
                        PB[_b].rearrange("p (k n) -> p k n", n=64)))
                    bank_free[b] = ("vV", ev)

            def phase_a(ei):
                xb = S4_XST + (ei % 2) * 3072
                hb = S4_HT + (ei % 2) * 1152
                xst_done = r.cnt["vV"]
                for ic in range(NIC):
                    gsl = ic % 2
                    gu_off = (P4_GU0, P4_GU1)[gsl]
                    if gu_free[gsl] is not None:
                        r.wait("sync", *gu_free[gsl])
                    cgu = r.dW(lambda e, _e=ei, _ic=ic, _g=gu_off: e.dma_start(
                        out=sl(_g, 4096).rearrange("p (k n) -> p k n", n=256),
                        in_=wguT_d[_e, :, _ic * 256:(_ic + 1) * 256]
                        .rearrange("(k p) n -> k p n", p=128).transpose([1, 0, 2])))
                    r.wait("tensor", "dW", cgu)
                    r.wait("tensor", "vV", xst_done)
                    bank_wait("tensor", 0)
                    bank_wait("tensor", 1)
                    for k in range(16):
                        r.t(lambda e, _k=k, _g=gu_off, _x=xb: e.matmul(
                            pb0[:, 0:CAP],
                            sl(_g + _k * 256, 128),
                            sl(_x + _k * 192, 192),
                            start=(_k == 0), stop=(_k == 15)), inc=(k == 15))
                    g_done = r.cnt["tM"]
                    for k in range(16):
                        r.t(lambda e, _k=k, _g=gu_off, _x=xb: e.matmul(
                            pb1[:, 0:CAP],
                            sl(_g + _k * 256 + 128, 128),
                            sl(_x + _k * 192, 192),
                            start=(_k == 0), stop=(_k == 15)), inc=(k == 15))
                    u_done = r.cnt["tM"]
                    gu_free[gsl] = ("tM", u_done)
                    r.wait("scalar", "tM", g_done)
                    sg = r.a(lambda e: e.activation(sl(S4_GS, CAP), pb0[:, 0:CAP], AF.Silu))
                    bank_free[0] = ("aA", sg)
                    r.wait("vector", "aA", sg)
                    r.wait("vector", "tM", u_done)
                    ht = r.v(lambda e, _ic=ic, _h=hb: e.tensor_mul(
                        sl(_h + _ic * CAP, CAP), sl(S4_GS, CAP), pb1[:, 0:CAP]))
                    bank_free[1] = ("vV", ht)

            def phase_b(ei):
                hb = S4_HT + (ei % 2) * 1152
                ht_all = r.cnt["vV"]
                r.wait("tensor", "vV", ht_all)
                YB = ((2, 3, 4, 5), (6, 7, 0, 1))
                for m in range(2):
                    for b in YB[m]:
                        bank_wait("tensor", b)
                for ic in range(NIC):
                    dsl = ic % 2
                    dw_off = (P4_DW0, P4_DW1)[dsl]
                    if dw_free[dsl] is not None:
                        r.wait("sync", *dw_free[dsl])
                    cdw = r.dW(lambda e, _e=ei, _ic=ic, _d=dw_off: e.dma_start(
                        out=sl(_d, DM), in_=wdwT_d[_e, _ic * 128:(_ic + 1) * 128, :]))
                    r.wait("tensor", "dW", cdw)
                    for m in range(2):
                        rows = 128 if m == 0 else CAP - 128
                        for nf in range(4):
                            r.t(lambda e, _m=m, _ic=ic, _nf=nf, _d=dw_off, _rw=rows,
                                _h=hb: e.matmul(
                                PB[YB[_m][_nf]][0:_rw, 0:512],
                                sl(_h + _ic * CAP + _m * 128, _rw),
                                sl(_d + _nf * 512, 512),
                                start=(_ic == 0), stop=(_ic == NIC - 1)),
                                inc=(m == 1 and nf == 3))
                    dw_free[dsl] = ("tM", r.cnt["tM"])
                y_done = r.cnt["tM"]
                for m in range(2):
                    rows = 128 if m == 0 else CAP - 128
                    r.wait("vector", "tM", y_done)
                    r.wait("vector", "dX")
                    ev = None
                    for nf in range(4):
                        ev = r.v(lambda e, _nf=nf, _m=m, _rw=rows: e.tensor_copy(
                            A[0:_rw, S4_YT + _nf * 512:S4_YT + (_nf + 1) * 512],
                            PB[YB[_m][_nf]][0:_rw, 0:512]))
                    for b in YB[m]:
                        bank_free[b] = ("vV", ev)
                    r.wait("gpsimd", "vV", ev)
                    r.dX(lambda e, _e=ei, _m=m, _rw=rows: e.dma_start(
                        out=ys_d[_e * CAP + _m * 128:_e * CAP + _m * 128 + _rw, :],
                        in_=A[0:_rw, S4_YT:S4_YT + DM]))

            load_xs(0)
            transposes(0)
            r.wait("sync", "tM", r.cnt["tM"])
            load_xs(1)
            for ei in range(E):
                phase_a(ei)
                if ei + 1 < E:
                    transposes(ei + 1)
                    if ei + 2 < E:
                        r.wait("sync", "tM", r.cnt["tM"])
                        load_xs(ei + 2)
                phase_b(ei)

            for m in range(NM):
                gb = (S4_GB0, S4_GB1)
                for j in range(4):
                    gbo = gb[j % 2]
                    r.wait("gpsimd", "vV")
                    gj = r.dX(lambda e, _m=m, _j=j, _g=gbo: e.indirect_dma_start(
                        out=sl(_g, DM), out_offset=None,
                        in_=ys_d,
                        in_offset=bass.IndirectOffsetOnAxis(IDXT[:, _m * 4 + _j:_m * 4 + _j + 1], 0)))
                    r.wait("vector", "dX", gj)
                    if j == 0:
                        r.v(lambda e, _m=m, _g=gbo: e.tensor_scalar(
                            sl(S4_CB, DM), sl(_g, DM),
                            sl(OFF_P4T + 32 + _m * 4, 1), None, op0=AL.mult))
                    else:
                        r.v(lambda e, _m=m, _j=j, _g=gbo: e.scalar_tensor_tensor(
                            sl(S4_CB, DM), sl(_g, DM),
                            sl(OFF_P4T + 32 + _m * 4 + _j, 1),
                            sl(S4_CB, DM), op0=AL.mult, op1=AL.add))
                moe_done = r.cnt["vV"]
                r.wait("scalar", "vV", moe_done)
                r.a(lambda e: e.activation(sl(S4_CB, DM), sl(S4_CB, DM), AF.Silu))
                g_done = r.cnt["aA"]
                row0 = m * 128
                r.dX(lambda e, _r=row0: e.dma_start(
                    out=sl(S4_CX, DM), in_=x_own_d[_r:_r + 128, :]))
                cs = r.dX(lambda e, _r=row0: e.dma_start(
                    out=sl(S4_CS, DM), in_=s_spill_d[_r:_r + 128, :]))
                r.wait("vector", "aA", g_done)
                r.wait("vector", "dX", cs)
                r.v(lambda e: e.tensor_mul(sl(S4_CB, DM), sl(S4_CB, DM), sl(S4_CS, DM)))
                r.v(lambda e: e.tensor_add(sl(S4_CB, DM), sl(S4_CB, DM), sl(S4_CX, DM)))
                o_done = r.cnt["vV"]
                r.wait("gpsimd", "vV", o_done)
                r.dX(lambda e, _r=row0: e.dma_start(
                    out=out_d[_r:_r + 128, :], in_=sl(S4_CB, DM)))
                r.wait("vector", "dX")

        def emit_ffn():
            gu_free = [None, None]
            dw_free = [None, None]
            for tc in range(2):
                tok0 = tc * 512
                r.wait("vector", "dX")
                for ei in range(E):
                    for ic in range(NIC):
                        gsl = ic % 2
                        gu_off = (P4_GU0, P4_GU1)[gsl]
                        if gu_free[gsl] is not None:
                            r.wait("sync", *gu_free[gsl])
                        cgu = r.dW(lambda e, _e=ei, _ic=ic, _g=gu_off: e.dma_start(
                            out=sl(_g, 4096).rearrange("p (k n) -> p k n", n=256),
                            in_=wguT_d[_e, :, _ic * 256:(_ic + 1) * 256]
                            .rearrange("(k p) n -> k p n", p=128).transpose([1, 0, 2])))
                        dw_off = (P4_DW0, P4_DW1)[gsl]
                        if dw_free[gsl] is not None:
                            r.wait("sync", *dw_free[gsl])
                        cdw = r.dW(lambda e, _e=ei, _ic=ic, _d=dw_off: e.dma_start(
                            out=sl(_d, DM), in_=wdwT_d[_e, _ic * 128:(_ic + 1) * 128, :]))
                        r.wait("tensor", "dW", cgu)
                        bank_wait("tensor", 0)
                        bank_wait("tensor", 1)
                        for k in range(16):
                            r.t(lambda e, _k=k, _g=gu_off, _t=tok0: e.matmul(
                                pb0[:, 0:512],
                                sl(_g + _k * 256, 128),
                                sl(OFF_XNT + _k * TPC + _t, 512),
                                start=(_k == 0), stop=(_k == 15)), inc=(k == 15))
                        g_done = r.cnt["tM"]
                        for k in range(16):
                            r.t(lambda e, _k=k, _g=gu_off, _t=tok0: e.matmul(
                                pb1[:, 0:512],
                                sl(_g + _k * 256 + 128, 128),
                                sl(OFF_XNT + _k * TPC + _t, 512),
                                start=(_k == 0), stop=(_k == 15)), inc=(k == 15))
                        u_done = r.cnt["tM"]
                        gu_free[gsl] = ("tM", u_done)
                        r.wait("scalar", "tM", g_done)
                        sg = r.a(lambda e: e.activation(sl(P4_GS, 512), pb0[:, 0:512], AF.Silu))
                        bank_free[0] = ("aA", sg)
                        r.wait("vector", "aA", sg)
                        r.wait("vector", "tM", u_done)
                        hdone = r.v(lambda e: e.tensor_mul(sl(P4_H, 512), sl(P4_GS, 512),
                                                           pb1[:, 0:512]))
                        bank_free[1] = ("vV", hdone)
                        r.wait("tensor", "vV", hdone)
                        r.wait("tensor", "dW", cdw)
                        ylast = None
                        for m in range(4):
                            yb = (2, 3, 4, 5) if m % 2 == 0 else (6, 7, 0, 1)
                            for bb in yb:
                                bank_wait("tensor", bb)
                            for nf in range(4):
                                r.t(lambda e, _m=m, _nf=nf, _d=dw_off, _b=yb[nf]: e.matmul(
                                    PB[_b][:, 0:512],
                                    sl(P4_H + _m * 128, 128),
                                    sl(_d + _nf * 512, 512),
                                    start=True, stop=True), inc=(nf == 3))
                            y_done = r.cnt["tM"]
                            dw_free[gsl] = ("tM", y_done)
                            r.wait("vector", "tM", y_done)
                            ev = None
                            for nf in range(4):
                                dst = P4_YAC + m * 2048 + nf * 512
                                wcol = OFF_WTOK + (tc * 4 + m) * E + ei
                                if ei == 0 and ic == 0:
                                    ev = r.v(lambda e, _d=dst, _b=yb[nf], _w=wcol: e.tensor_scalar(
                                        sl(_d, 512), PB[_b][:, 0:512], sl(_w, 1), None,
                                        op0=AL.mult))
                                else:
                                    ev = r.v(lambda e, _d=dst, _b=yb[nf], _w=wcol:
                                             e.scalar_tensor_tensor(
                                                 sl(_d, 512), PB[_b][:, 0:512], sl(_w, 1),
                                                 sl(_d, 512), op0=AL.mult, op1=AL.add))
                            for bb in yb:
                                bank_free[bb] = ("vV", ev)
                for m in range(4):
                    gm = r.cnt["vV"]
                    r.wait("scalar", "vV", gm)
                    r.a(lambda e, _m=m: e.activation(
                        sl(P4_YAC + _m * 2048, DM), sl(P4_YAC + _m * 2048, DM), AF.Silu))
                    g_done = r.cnt["aA"]
                    row0 = tok0 + m * 128
                    r.dX(lambda e, _r=row0: e.dma_start(
                        out=sl(P4_XTI, DM), in_=x_own_d[_r:_r + 128, :]))
                    cs = r.dX(lambda e, _r=row0: e.dma_start(
                        out=sl(P4_STI, DM), in_=s_spill_d[_r:_r + 128, :]))
                    r.wait("vector", "aA", g_done)
                    r.wait("vector", "dX", cs)
                    r.v(lambda e, _m=m: e.tensor_mul(
                        sl(P4_YAC + _m * 2048, DM), sl(P4_YAC + _m * 2048, DM), sl(P4_STI, DM)))
                    r.v(lambda e, _m=m: e.tensor_add(
                        sl(P4_YAC + _m * 2048, DM), sl(P4_YAC + _m * 2048, DM), sl(P4_XTI, DM)))
                    o_done = r.cnt["vV"]
                    r.wait("gpsimd", "vV", o_done)
                    r.dX(lambda e, _r=row0, _m=m: e.dma_start(
                        out=out_d[_r:_r + 128, :], in_=sl(P4_YAC + _m * 2048, DM)))

        emit_consts()
        r.barrier()
        for m in range(NM):
            emit_ln_tile(x_pre_d, OFF_XNTP, m, spill=False, bank_base=0)
        r.barrier()
        for h in range(H):
            emit_pre_head(h)
            if h % 2 == 1:
                emit_ln_tile(x_own_d, OFF_XNT, h // 2, spill=SPARSE, bank_base=6)
        r.barrier()
        if debug:
            r.dX(lambda e: e.dma_start(
                out=dbg["xnT_dump"].transpose([1, 0, 2]),
                in_=sl(OFF_XNT, 16 * TPC).rearrange("p (k n) -> p k n", n=TPC)))
            r.barrier()
        if SPARSE:
            emit_router()
            r.barrier()
            if debug:
                r.dX(lambda e: e.dma_start(
                    out=dbg["wtok_dump"].transpose([1, 0, 2]),
                    in_=sl(OFF_WTOK, NM * E).rearrange("p (m n) -> p m n", n=E)))
                r.barrier()
            emit_tables_scatter()
            r.barrier()
        emit_attention()
        r.barrier()
        if debug:
            r.dX(lambda e: e.dma_start(out=dbg["kv_dump"][:], in_=sl(OFF_KV, SZ_KV)))
            r.dX(lambda e: e.dma_start(out=dbg["s_dump"][:], in_=s_spill_d[:]))
            r.barrier()
        if SPARSE:
            emit_ffn_sparse()
        else:
            emit_router()
            r.barrier()
            if debug:
                r.dX(lambda e: e.dma_start(
                    out=dbg["wtok_dump"].transpose([1, 0, 2]),
                    in_=sl(OFF_WTOK, NM * E).rearrange("p (m n) -> p m n", n=E)))
                r.barrier()
            emit_ffn()
        fin_dX = r.cnt["dX"]
        fin_dW = r.cnt["dW"]
        r.rec["gpsimd"].append(lambda EE, SS, _v=fin_dX: EE["gpsimd"].wait_ge(SS["dX"], _v))
        r.rec["sync"].append(lambda EE, SS, _v=fin_dW: EE["sync"].wait_ge(SS["dW"], _v))

        def runner(eng_name):
            def go(eng_obj):
                EE = {eng_name: eng_obj}
                for fn in r.rec[eng_name]:
                    fn(EE, SEMS)
            return go

        block.sync(runner("sync"))
        block.gpsimd(runner("gpsimd"))
        block.vector(runner("vector"))
        block.scalar(runner("scalar"))
        block.tensor(runner("tensor"))

    return nc


def _prep_inputs(L0_attn_x, L0_attn_slope_rate, L0_attn_W_qkvu, L0_attn_W_out,
                 L0_ffn_router_w, L0_ffn_gate_w, L0_ffn_up_w, L0_ffn_down_w):
    x = np.ascontiguousarray(np.asarray(L0_attn_x, np.float32))
    slope = np.ascontiguousarray(np.asarray(L0_attn_slope_rate, np.float32).reshape(H))
    wqkvuT = np.ascontiguousarray(np.asarray(L0_attn_W_qkvu, np.float32).T)
    woutT = np.ascontiguousarray(np.asarray(L0_attn_W_out, np.float32).T)
    wrT = np.ascontiguousarray(np.asarray(L0_ffn_router_w, np.float32).T)
    gate = np.asarray(L0_ffn_gate_w, np.float32)
    up = np.asarray(L0_ffn_up_w, np.float32)
    down = np.asarray(L0_ffn_down_w, np.float32)
    gT = gate.transpose(0, 2, 1).reshape(E, DM, NIC, 128)
    uT = up.transpose(0, 2, 1).reshape(E, DM, NIC, 128)
    wguT = np.ascontiguousarray(
        np.concatenate([gT[:, :, :, None, :], uT[:, :, :, None, :]], axis=3)
        .reshape(E, DM, 2 * I))
    wdwT = np.ascontiguousarray(down.transpose(0, 2, 1))
    zeros = np.zeros((TPC, DM), np.float32)
    in_maps = []
    for c in range(8):
        b, half = c // 2, c % 2
        x_own = np.ascontiguousarray(x[b, half * TPC:(half + 1) * TPC, :])
        x_pre = np.ascontiguousarray(x[b, 0:TPC, :]) if half == 1 else zeros
        in_maps.append(dict(x_own=x_own, x_pre=x_pre, slope=slope, wqkvuT=wqkvuT,
                            woutT=woutT, wrT=wrT, wguT=wguT, wdwT=wdwT))
    return in_maps


def kernel(L0_attn_x, L0_attn_slope_rate, L0_attn_W_qkvu, L0_attn_W_out,
           L0_ffn_router_w, L0_ffn_gate_w, L0_ffn_up_w, L0_ffn_down_w):
    in_maps = _prep_inputs(L0_attn_x, L0_attn_slope_rate, L0_attn_W_qkvu, L0_attn_W_out,
                           L0_ffn_router_w, L0_ffn_gate_w, L0_ffn_up_w, L0_ffn_down_w)
    nc = build_nc(debug=False)
    res = run_bass_kernel_spmd(nc, in_maps, core_ids=list(range(8)))
    out = np.empty((B, N, DM), np.float32)
    for c in range(8):
        b, half = c // 2, c % 2
        out[b, half * TPC:(half + 1) * TPC, :] = res.results[c]["out"]
    return out


# version 11
# speedup vs baseline: 1.0077x; 1.0077x over previous
import numpy as np
import concourse.bass as bass
import concourse.mybir as mybir
from concourse.bass_utils import run_bass_kernel_spmd

F32 = mybir.dt.float32
AF = mybir.ActivationFunctionType
AX = mybir.AxisListType
AL = mybir.AluOpType

B, N, DM = 4, 2048, 2048
H, DH = 16, 128
BLK = 256
TPC = 1024
NBLK = TPC // BLK
NM = TPC // 128
G4 = 4 * DM
E, I = 32, 768
NIC = I // 128
LN_EPS = 1e-5
RMS_EPS = 1e-6
SPARSE = True
CAP = 192

ENGS = ("sync", "gpsimd", "vector", "scalar", "tensor")
SEM_PROD = {"dW": "sync", "dX": "gpsimd", "tM": "tensor", "vV": "vector",
            "aA": "scalar", "gG": "gpsimd"}
ASYNC_SEMS = ("dW", "dX")


class Rec:

    def __init__(self):
        self.rec = {e: [] for e in ENGS}
        self.cnt = {s: 0 for s in SEM_PROD}
        self.waited = {}

    def wait(self, eng, sem, val=None, force=False):
        if val is None:
            val = self.cnt[sem]
        if val <= 0:
            return
        if SEM_PROD[sem] == eng and sem not in ASYNC_SEMS and not force:
            return
        key = (eng, sem)
        if self.waited.get(key, 0) >= val:
            return
        self.waited[key] = val
        self.rec[eng].append(lambda EE, SS, _e=eng, _s=sem, _v=val: EE[_e].wait_ge(SS[_s], _v))

    def ins(self, eng, fn, sem=None, d=1):
        if sem is not None:
            self.cnt[sem] += d
        def go(EE, SS, _fn=fn, _sem=sem, _d=d, _eng=eng):
            rr = _fn(EE[_eng])
            if _sem is not None:
                rr.then_inc(SS[_sem], _d)
        self.rec[eng].append(go)
        return self.cnt[sem] if sem is not None else None

    def v(self, fn):
        self.wait("vector", "vV", force=True)
        return self.ins("vector", fn, "vV")

    def a(self, fn):
        self.wait("scalar", "aA", force=True)
        return self.ins("scalar", fn, "aA")

    def g(self, fn):
        self.wait("gpsimd", "gG", force=True)
        return self.ins("gpsimd", fn, "gG")

    def t(self, fn, inc=False):
        return self.ins("tensor", fn, "tM" if inc else None)

    def dW(self, fn):
        self.wait("sync", "dW", force=True)
        return self.ins("sync", fn, "dW", 16)

    def dX(self, fn):
        self.wait("gpsimd", "dX", force=True)
        return self.ins("gpsimd", fn, "dX", 16)

    def barrier(self):
        for eng in ENGS:
            for sem in SEM_PROD:
                self.wait(eng, sem)


def ceil128(x):
    return (x + 127) // 128 * 128


def build_nc(debug=False):
    nc = bass.Bass("TRN2", target_bir_lowering=False)

    x_own_d = nc.dram_tensor("x_own", [TPC, DM], F32, kind="ExternalInput").ap()
    x_pre_d = nc.dram_tensor("x_pre", [TPC, DM], F32, kind="ExternalInput").ap()
    slope_d = nc.dram_tensor("slope", [H], F32, kind="ExternalInput").ap()
    wqkvuT_d = nc.dram_tensor("wqkvuT", [DM, G4], F32, kind="ExternalInput").ap()
    woutT_d = nc.dram_tensor("woutT", [DM, DM], F32, kind="ExternalInput").ap()
    wrT_d = nc.dram_tensor("wrT", [DM, E], F32, kind="ExternalInput").ap()
    wguT_d = nc.dram_tensor("wguT", [E, DM, 2 * I], F32, kind="ExternalInput").ap()
    wdwT_d = nc.dram_tensor("wdwT", [E, I, DM], F32, kind="ExternalInput").ap()
    out_d = nc.dram_tensor("out", [TPC, DM], F32, kind="ExternalOutput").ap()
    s_spill_d = nc.dram_tensor("s_spill", [TPC, DM], F32, kind="Internal").ap()
    xs_d = nc.dram_tensor("xs_slots", [E * CAP, DM], F32, kind="Internal").ap()
    ys_d = nc.dram_tensor("ys_slots", [E * CAP, DM], F32, kind="Internal").ap()
    xn_d = nc.dram_tensor("xn_rows", [TPC, DM], F32, kind="Internal").ap()
    dbg = {}
    if debug:
        for name, shape in (("xnT_dump", [16, 128, TPC]), ("kv_dump", [128, H * DH]),
                            ("s_dump", [TPC, DM]), ("wtok_dump", [NM, 128, E]),
                            ("kvblk_dump", [NBLK, 128, H * DH]),
                            ("p0dbg", [4, 128, 512])):
            dbg[name] = nc.dram_tensor(name, shape, F32, kind="ExternalOutput").ap()

    OFF_XNT = 0
    SZ_XNT = 16 * TPC
    OFF_KV = OFF_XNT + SZ_XNT
    SZ_KV = H * DH
    OFF_WTOK = OFF_KV + SZ_KV
    OFF_P4T = OFF_WTOK + NM * E
    OFF_CONST = OFF_P4T + 64
    C_S = OFF_CONST
    C_NS = C_S + 16
    C_BD = C_NS + 16
    C_QD = C_BD + 16
    C_KD = C_QD + 32
    C_IOTP = C_KD + 32
    C_BIAS = C_IOTP + 8
    C_IDX0 = C_BIAS + 8
    C_IDX1 = C_IDX0 + 256
    C_RIDX0 = C_IDX1 + 256
    C_RIDX1 = C_RIDX0 + 256
    C_MASK0 = C_RIDX1 + 256
    C_MASK1 = C_MASK0 + 256
    C_IDENT = C_MASK1 + 256
    OFF_REST = ceil128(C_IDENT + 128)
    R = OFF_REST
    OFF_XNTP = R
    P0_XT0 = R + 16384 + 6144
    P0_XN = P0_XT0 + DM
    P0_ST = P0_XN + DM
    P0_END = P0_ST + 128
    P1_WK = R + 16384
    P1_WV = P1_WK + 2048
    P1_KKD = P1_WV + 2048
    P1_VST = P1_KKD + 1024
    P1_END = P1_VST + 1024
    P2_W0 = R
    P2_W1 = P2_W0 + 2048
    P2_EW0 = R
    P2_EW1 = R + 4096
    P2_HB0 = R + 4096
    P2_HB1 = P2_HB0 + 5120
    HB_QT = 0
    HB_KT = 1024
    HB_KTOK = 2048
    HB_VTOK = 3072
    HB_VT = 4096
    HB_D = 4096
    HB_QKD = 4608
    P2_O = P2_HB1 + 5120
    P2_RMS = P2_O + 16384
    P2_SSQ = P2_RMS + 8
    P2_ST = P2_SSQ + 8
    P2_OUT = P2_HB1
    P2_SB = P2_HB1 + 2048
    P2_END = ceil128(P2_ST + 48)
    P35 = R + 16384
    T_WR = P35
    P3_LG = T_WR + 512
    P3_SC = P3_LG + 64
    P3_M8 = P3_SC + 64
    P3_ST = P3_M8 + 16
    T_WT = ceil128(P3_ST + 16)
    T_MASK = T_WT + 1024
    T_CUMA = T_MASK + 1024
    T_CUMB = T_CUMA + 1024
    T_POSM = T_CUMB + 1024
    T_CNT = T_POSM + 1024
    T_POSTOK = T_CNT + 128
    T_M8B = T_POSTOK + 256
    T_ONE = T_M8B + 8
    P3_END = T_ONE + 64
    P4_GU0 = R
    P4_GU1 = P4_GU0 + 4096
    P4_DW0 = P4_GU1 + 4096
    P4_DW1 = P4_DW0 + DM
    P4_H = P4_DW1 + DM
    P4_GS = P4_H + 512
    P4_YAC = P4_GS + 512
    P4_XTI = P4_YAC + 8192
    P4_STI = P4_XTI + DM
    P4_END = P4_STI + DM
    S4_XS = P4_DW1 + DM
    S4_XST = S4_XS + 4096
    S4_HT = S4_XST + 6144
    S4_GS = S4_HT + 2304
    S4_YT = S4_GS + 192
    S4_END = S4_YT + 2048
    S4_GB0 = S4_XS
    S4_GB1 = S4_GB0 + 2048
    S4_CB = S4_GB1 + 2048
    S4_CX = S4_CB + 2048
    S4_CS = S4_CX + 2048
    P4_END = max(P4_END, S4_END, S4_CS + 2048)

    ARENA = ceil128(max(P0_END, P1_END, P2_END, P3_END, P4_END))
    assert ARENA <= 53000, f"arena {ARENA} cols exceeds SBUF"

    r = Rec()

    with (
        nc.sbuf_tensor("A", [128, ARENA], F32) as A,
        nc.sbuf_tensor("IDXT", [128, 64], mybir.dt.int32) as IDXT,
        nc.psum_tensor("pb0", [128, 512], F32) as pb0,
        nc.psum_tensor("pb1", [128, 512], F32) as pb1,
        nc.psum_tensor("pb2", [128, 512], F32) as pb2,
        nc.psum_tensor("pb3", [128, 512], F32) as pb3,
        nc.psum_tensor("pb4", [128, 512], F32) as pb4,
        nc.psum_tensor("pb5", [128, 512], F32) as pb5,
        nc.psum_tensor("pb6", [128, 512], F32) as pb6,
        nc.psum_tensor("pb7", [128, 512], F32) as pb7,
        nc.semaphore("dW") as sem_dW,
        nc.semaphore("dX") as sem_dX,
        nc.semaphore("tM") as sem_tM,
        nc.semaphore("vV") as sem_vV,
        nc.semaphore("aA") as sem_aA,
        nc.semaphore("gG") as sem_gG,
        nc.Block() as block,
    ):
        PB = [pb0, pb1, pb2, pb3, pb4, pb5, pb6, pb7]
        SEMS = {"dW": sem_dW, "dX": sem_dX, "tM": sem_tM, "vV": sem_vV,
                "aA": sem_aA, "gG": sem_gG}

        def sl(off, n):
            return A[:, off:off + n]

        bank_free = [None] * 8
        shared = {}

        def bank_wait(eng, b):
            if bank_free[b] is not None:
                r.wait(eng, *bank_free[b])

        def wchunk_src(dram, col0, ncols):
            return (dram[:, col0:col0 + ncols]
                    .rearrange("(k p) n -> k p n", p=128).transpose([1, 0, 2]))

        def emit_consts():
            r.dW(lambda e: e.dma_start(out=sl(C_S, 16),
                                       in_=slope_d[None, :].broadcast_to([128, H])))
            r.wait("vector", "dW")
            r.v(lambda e: e.tensor_scalar_mul(sl(C_NS, 16), sl(C_S, 16), -1.0))
            r.wait("scalar", "vV")
            r.a(lambda e: e.activation(sl(C_BD, 16), sl(C_S, 16), AF.Exp, scale=-float(BLK)))
            r.g(lambda e: e.iota(sl(C_IOTP, 1), pattern=[[0, 1]], base=0,
                                 channel_multiplier=1, allow_small_or_imprecise_dtypes=True))
            r.g(lambda e: e.iota(sl(C_IDX0, 256), pattern=[[1, 256]], base=0,
                                 channel_multiplier=-1, allow_small_or_imprecise_dtypes=True))
            r.g(lambda e: e.iota(sl(C_IDX1, 256), pattern=[[1, 256]], base=-128,
                                 channel_multiplier=-1, allow_small_or_imprecise_dtypes=True))
            r.v(lambda e: e.memset(sl(C_MASK0, 512 + 128), 1.0))
            r.wait("gpsimd", "vV")
            r.g(lambda e: e.affine_select(sl(C_MASK0, 256), sl(C_MASK0, 256),
                                          pattern=[[1, 256]], base=0, channel_multiplier=-1,
                                          compare_op=AL.is_ge, fill=0.0))
            r.g(lambda e: e.affine_select(sl(C_MASK1, 256), sl(C_MASK1, 256),
                                          pattern=[[1, 256]], base=-128, channel_multiplier=-1,
                                          compare_op=AL.is_ge, fill=0.0))
            r.g(lambda e: e.affine_select(sl(C_IDENT, 128), sl(C_IDENT, 128),
                                          pattern=[[-1, 128]], base=0, channel_multiplier=1,
                                          compare_op=AL.is_equal, fill=0.0))
            r.wait("vector", "gG")
            r.v(lambda e: e.tensor_scalar_max(sl(C_RIDX0, 512), sl(C_IDX0, 512), 0.0))
            for h in range(H):
                for c in range(2):
                    r.wait("vector", "aA")
                    r.wait("vector", "gG")
                    r.v(lambda e, _h=h, _c=c: e.tensor_scalar_mul(
                        sl(C_BIAS, 1), sl(C_NS + _h, 1), float(128 * _c + 1)))
                    r.v(lambda e, _h=h, _c=c: e.tensor_scalar_mul(
                        sl(C_BIAS + 1, 1), sl(C_S + _h, 1), float(128 * _c - 255)))
                    r.wait("scalar", "vV")
                    r.wait("scalar", "gG")
                    r.a(lambda e, _h=h, _c=c: e.activation(
                        sl(C_QD + 2 * _h + _c, 1), sl(C_IOTP, 1), AF.Exp,
                        scale=sl(C_NS + _h, 1), bias=sl(C_BIAS, 1)))
                    r.a(lambda e, _h=h, _c=c: e.activation(
                        sl(C_KD + 2 * _h + _c, 1), sl(C_IOTP, 1), AF.Exp,
                        scale=sl(C_S + _h, 1), bias=sl(C_BIAS + 1, 1)))
            r.v(lambda e: e.memset(sl(OFF_KV, SZ_KV), 0.0))
            if SPARSE:
                zt = r.v(lambda e: e.memset(sl(P0_XN, DM), 0.0))
                r.wait("sync", "vV", zt)
                for i in range(E * CAP // 128):
                    r.dW(lambda e, _i=i: e.dma_start(
                        out=xs_d[_i * 128:(_i + 1) * 128, :], in_=sl(P0_XN, DM)))

        ln_state = {"buf_release": 0, "xn_spill": 0}

        def emit_ln_tile(src_d, xnt_off, m, spill, bank_base):
            st = sl(P0_ST, 8)
            if ln_state["buf_release"]:
                r.wait("gpsimd", "vV", ln_state["buf_release"])
            ld = r.dX(lambda e, _m=m: e.dma_start(
                out=sl(P0_XT0, DM), in_=src_d[_m * 128:(_m + 1) * 128, :]))
            r.wait("vector", "dX", ld)
            r.v(lambda e: e.reduce_sum(st[:, 0:1], sl(P0_XT0, DM), axis=AX.X))
            r.v(lambda e: e.tensor_scalar_mul(st[:, 0:1], st[:, 0:1], 1.0 / DM))
            r.wait("scalar", "dX", max(ld, ln_state["xn_spill"]))
            r.wait("scalar", "tM")
            r.a(lambda e: e.activation(sl(P0_XN, DM), sl(P0_XT0, DM), AF.Square))
            r.wait("vector", "aA")
            r.v(lambda e: e.reduce_sum(st[:, 1:2], sl(P0_XN, DM), axis=AX.X))
            r.v(lambda e: e.tensor_scalar(st[:, 1:2], st[:, 1:2], 1.0 / DM, LN_EPS,
                                          op0=AL.mult, op1=AL.add))
            r.v(lambda e: e.tensor_mul(st[:, 2:3], st[:, 0:1], st[:, 0:1]))
            r.v(lambda e: e.tensor_sub(st[:, 1:2], st[:, 1:2], st[:, 2:3]))
            r.wait("scalar", "vV")
            r.a(lambda e: e.activation(st[:, 3:4], st[:, 1:2], AF.Sqrt))
            r.wait("vector", "aA")
            r.v(lambda e: e.reciprocal(st[:, 4:5], st[:, 3:4]))
            r.v(lambda e: e.tensor_mul(st[:, 5:6], st[:, 0:1], st[:, 4:5]))
            r.v(lambda e: e.tensor_scalar(sl(P0_XN, DM), sl(P0_XT0, DM),
                                          st[:, 4:5], st[:, 5:6],
                                          op0=AL.mult, op1=AL.subtract))
            xn_done = r.cnt["vV"]
            ln_state["buf_release"] = xn_done
            if spill:
                r.wait("gpsimd", "vV", xn_done)
                ln_state["xn_spill"] = r.dX(lambda e, _m=m: e.dma_start(
                    out=xn_d[_m * 128:(_m + 1) * 128, :], in_=sl(P0_XN, DM)))
            r.wait("tensor", "vV", xn_done)
            r.wait("tensor", "gG")
            for grp in range(4):
                b = bank_base + grp % 2
                bank_wait("tensor", b)
                for f4 in range(4):
                    f = grp * 4 + f4
                    r.t(lambda e, _f=f, _f4=f4, _b=b: e.transpose(
                        PB[_b][:, _f4 * 128:(_f4 + 1) * 128],
                        sl(P0_XN + _f * 128, 128), sl(C_IDENT, 128)),
                        inc=(f4 == 3))
                tdone = r.cnt["tM"]
                r.wait("vector", "tM", tdone)
                d0 = xnt_off + (grp * 4) * TPC + m * 128
                ev = r.v(lambda e, _d=d0, _b=b: e.tensor_copy(
                    A[:, _d:_d + 4 * TPC].rearrange("p (k n) -> p k n", n=TPC)[:, :, 0:128],
                    PB[_b].rearrange("p (k n) -> p k n", n=128)))
                bank_free[b] = ("vV", ev)

        p1_state = {"w_release": 0}

        def emit_pre_head(h):
            if p1_state["w_release"]:
                r.wait("sync", "tM", p1_state["w_release"])
            r.dW(lambda e, _h=h: e.dma_start(
                out=sl(P1_WK, 2048).rearrange("p (k n) -> p k n", n=128),
                in_=wchunk_src(wqkvuT_d, DM + _h * DH, 128)))
            r.dW(lambda e, _h=h: e.dma_start(
                out=sl(P1_WV, 2048).rearrange("p (k n) -> p k n", n=128),
                in_=wchunk_src(wqkvuT_d, 2 * DM + _h * DH, 128)))
            w_done = r.cnt["dW"]
            r.wait("tensor", "dW", w_done)
            for (woff, store, is_k) in ((P1_WK, P1_KKD, True), (P1_WV, P1_VST, False)):
                for m in range(NM):
                    b = (0 if is_k else 2) + m % 2
                    bank_wait("tensor", b)
                    for k in range(16):
                        r.t(lambda e, _k=k, _m=m, _w=woff, _b=b: e.matmul(
                            PB[_b][:, 0:128],
                            sl(OFF_XNTP + _k * TPC + _m * 128, 128),
                            sl(_w + _k * 128, 128),
                            start=(_k == 0), stop=(_k == 15)), inc=(k == 15))
                    tdone = r.cnt["tM"]
                    if is_k:
                        r.wait("scalar", "tM", tdone)
                        sdone = r.a(lambda e, _m=m, _b=b: e.activation(
                            sl(P1_KKD + _m * 128, 128), PB[_b][:, 0:128], AF.Silu))
                        bank_free[b] = ("aA", sdone)
                        r.wait("vector", "aA", sdone)
                        r.v(lambda e, _m=m, _h=h: e.tensor_scalar(
                            sl(P1_KKD + _m * 128, 128), sl(P1_KKD + _m * 128, 128),
                            sl(C_KD + 2 * _h + (_m % 2), 1), None, op0=AL.mult))
                    else:
                        r.wait("vector", "tM", tdone)
                        vdone = r.v(lambda e, _m=m, _b=b: e.tensor_copy(
                            sl(P1_VST + _m * 128, 128), PB[_b][:, 0:128]))
                        bank_free[b] = ("vV", vdone)
            p1_state["w_release"] = r.cnt["tM"]
            proj_done = r.cnt["vV"]
            r.wait("tensor", "vV", proj_done)
            for blk in range(NBLK):
                b = 4 + blk % 2
                bank_wait("tensor", b)
                for mc in range(2):
                    m = blk * 2 + mc
                    r.t(lambda e, _m=m, _b=b: e.matmul(
                        PB[_b][:, 0:128],
                        sl(P1_KKD + _m * 128, 128),
                        sl(P1_VST + _m * 128, 128),
                        start=(_m % 2 == 0), stop=(_m % 2 == 1)), inc=(mc == 1))
                tdone = r.cnt["tM"]
                r.wait("vector", "tM", tdone)
                vdone = r.v(lambda e, _h=h, _b=b: e.scalar_tensor_tensor(
                    sl(OFF_KV + _h * 128, 128), sl(OFF_KV + _h * 128, 128),
                    sl(C_BD + _h, 1), PB[_b][:, 0:128], op0=AL.mult, op1=AL.add))
                bank_free[b] = ("vV", vdone)


        slot_free = [None, None]

        def load_w_chunk(slot_idx, dram, col0, ncols):
            woff = (P2_W0, P2_W1)[slot_idx]
            if slot_free[slot_idx] is not None:
                r.wait("sync", *slot_free[slot_idx])
            cnt = r.dW(lambda e, _w=woff, _d=dram, _c=col0, _n=ncols: e.dma_start(
                out=sl(_w, 16 * _n).rearrange("p (k n) -> p k n", n=_n),
                in_=wchunk_src(_d, _c, _n)))
            return woff, cnt

        def emit_attention():
            if SPARSE:
                r.wait("gpsimd", "vV", shared["cast_done"])
                for m in range(NM):
                    r.dX(lambda e, _m=m: e.dma_start(
                        out=sl(P2_OUT, DM), in_=xn_d[_m * 128:(_m + 1) * 128, :]))
                    for j in range(4):
                        r.dX(lambda e, _m=m, _j=j: e.indirect_dma_start(
                            out=xs_d,
                            out_offset=bass.IndirectOffsetOnAxis(
                                IDXT[:, _m * 4 + _j:_m * 4 + _j + 1], 0),
                            in_=sl(P2_OUT, DM), in_offset=None))
                shared["scat_end"] = r.cnt["dX"]

            def proj_stage(h, stage):
                hb = (P2_HB0, P2_HB1)[h % 2]
                if stage < 3:
                    col0, dst = ((h * DH, hb + HB_QT), (DM + h * DH, hb + HB_KT),
                                 (2 * DM + h * DH, hb + HB_VT))[stage]
                    wq, cq = load_w_chunk(stage % 2, wqkvuT_d, col0, DH)
                    r.wait("tensor", "dW", cq)
                    bank_wait("tensor", 4)
                    bank_wait("tensor", 5)
                    for k in range(16):
                        for tc in range(2):
                            r.t(lambda e, _k=k, _tc=tc, _w=wq: e.matmul(
                                PB[4 + _tc][:, 0:512],
                                sl(_w + _k * DH, DH),
                                sl(OFF_XNT + _k * TPC + _tc * 512, 512),
                                start=(_k == 0), stop=(_k == 15)),
                                inc=(k == 15 and tc == 1))
                    pj_done = r.cnt["tM"]
                    slot_free[stage % 2] = ("tM", pj_done)
                    if stage < 2:
                        r.wait("scalar", "tM", pj_done)
                        ev = None
                        for tc in range(2):
                            ev = r.a(lambda e, _tc=tc, _d=dst: e.activation(
                                sl(_d + _tc * 512, 512), PB[4 + _tc][:, 0:512], AF.Silu))
                        bank_free[4] = ("aA", ev)
                        bank_free[5] = ("aA", ev)
                    else:
                        r.wait("vector", "tM", pj_done)
                        ev = None
                        for tc in range(2):
                            ev = r.v(lambda e, _tc=tc, _d=dst: e.tensor_copy(
                                sl(_d + _tc * 512, 512), PB[4 + _tc][:, 0:512]))
                        bank_free[4] = ("vV", ev)
                        bank_free[5] = ("vV", ev)
                    if stage == 1:
                        r.wait("tensor", "aA")
                        for grp in range(2):
                            b = 6 + grp % 2
                            bank_wait("tensor", b)
                            for q4 in range(4):
                                m = grp * 4 + q4
                                r.t(lambda e, _m=m, _q=q4, _b=b, _s=hb + HB_KT: e.transpose(
                                    PB[_b][:, _q * 128:(_q + 1) * 128],
                                    sl(_s + _m * 128, 128), sl(C_IDENT, 128)),
                                    inc=(q4 == 3))
                            td = r.cnt["tM"]
                            r.wait("vector", "tM", td)
                            ev = r.v(lambda e, _d=hb + HB_KTOK, _g=grp, _b=b: e.tensor_copy(
                                sl(_d + _g * 512, 512), PB[_b][:, 0:512]))
                            bank_free[b] = ("vV", ev)
                        for mt in range(NM):
                            r.v(lambda e, _mt=mt, _h=h, _d=hb + HB_KTOK: e.tensor_scalar(
                                sl(_d + _mt * 128, 128), sl(_d + _mt * 128, 128),
                                sl(C_KD + 2 * _h + (_mt % 2), 1), None, op0=AL.mult))
                else:
                    r.wait("tensor", "vV")
                    for grp in range(2):
                        b = 6 + grp % 2
                        bank_wait("tensor", b)
                        for q4 in range(4):
                            m = grp * 4 + q4
                            r.t(lambda e, _m=m, _q=q4, _b=b, _s=hb + HB_VT: e.transpose(
                                PB[_b][:, _q * 128:(_q + 1) * 128],
                                sl(_s + _m * 128, 128), sl(C_IDENT, 128)),
                                inc=(q4 == 3))
                        td = r.cnt["tM"]
                        r.wait("vector", "tM", td)
                        ev = r.v(lambda e, _d=hb + HB_VTOK, _g=grp, _b=b: e.tensor_copy(
                            sl(_d + _g * 512, 512), PB[_b][:, 0:512]))
                        bank_free[b] = ("vV", ev)
                    r.wait("scalar", "vV")
                    for c in range(2):
                        r.a(lambda e, _c=c, _h=h, _d=hb + HB_D: e.activation(
                            sl(_d + _c * 256, 256), sl(C_RIDX0 + _c * 256, 256),
                            AF.Exp, scale=sl(C_NS + _h, 1)))
                    r.wait("vector", "aA")
                    r.v(lambda e, _d=hb + HB_D: e.tensor_mul(sl(_d, 512), sl(_d, 512),
                                                             sl(C_MASK0, 512)))

            def scan_block(h, blk):
                hb = (P2_HB0, P2_HB1)[h % 2]
                t0 = blk * BLK
                r.wait("tensor", "vV")
                bank_wait("tensor", 1)
                for sc in range(2):
                    r.t(lambda e, _sc=sc, _t=t0, _hb=hb: e.matmul(
                        pb1[:, _sc * 256:(_sc + 1) * 256],
                        sl(_hb + HB_KT + _t + _sc * 128, 128),
                        sl(_hb + HB_QT + _t, 256),
                        start=True, stop=True), inc=(sc == 1))
                qk_done = r.cnt["tM"]
                r.wait("vector", "tM", qk_done)
                qkd = r.v(lambda e, _hb=hb: e.tensor_mul(
                    sl(_hb + HB_QKD, 512), pb1[:, 0:512], sl(_hb + HB_D, 512)))
                bank_free[1] = ("vV", qkd)
                bank_wait("tensor", 0)
                for mc in range(2):
                    r.t(lambda e, _mc=mc, _h=h, _t=t0, _hb=hb: e.matmul(
                        pb0[:, _mc * 128:(_mc + 1) * 128],
                        sl(_hb + HB_QT + _t + _mc * 128, 128),
                        sl(OFF_KV + _h * 128, 128),
                        start=True, stop=True), inc=(mc == 1))
                bank_wait("tensor", 3)
                for sc in range(2):
                    r.t(lambda e, _sc=sc, _b=blk, _hb=hb: e.matmul(
                        pb3[:, 0:128],
                        sl(_hb + HB_KTOK + (_b * 2 + _sc) * 128, 128),
                        sl(_hb + HB_VTOK + (_b * 2 + _sc) * 128, 128),
                        start=(_sc == 0), stop=(_sc == 1)), inc=(sc == 1))
                psC_done = r.cnt["tM"]
                r.wait("tensor", "vV", qkd)
                bank_wait("tensor", 2)
                for mc in range(2):
                    for sc in range(2):
                        r.t(lambda e, _mc=mc, _sc=sc, _b=blk, _hb=hb: e.matmul(
                            pb2[:, _mc * 128:(_mc + 1) * 128],
                            sl(_hb + HB_QKD + _sc * 256 + _mc * 128, 128),
                            sl(_hb + HB_VTOK + (_b * 2 + _sc) * 128, 128),
                            start=(_sc == 0), stop=(_sc == 1)),
                            inc=(mc == 1 and sc == 1))
                psB_done = r.cnt["tM"]
                r.wait("vector", "tM", psB_done)
                ev = None
                for mc in range(2):
                    m = blk * 2 + mc
                    dst = P2_O + m * 2048 + h * 128
                    r.v(lambda e, _d=dst, _mc=mc, _h=h: e.tensor_scalar(
                        sl(_d, 128), pb0[:, _mc * 128:(_mc + 1) * 128],
                        sl(C_QD + 2 * _h + _mc, 1), None, op0=AL.mult))
                    ev = r.v(lambda e, _d=dst, _mc=mc: e.tensor_add(
                        sl(_d, 128), sl(_d, 128), pb2[:, _mc * 128:(_mc + 1) * 128]))
                bank_free[0] = ("vV", ev)
                bank_free[2] = ("vV", ev)
                r.wait("vector", "tM", psC_done)
                kvu = r.v(lambda e, _h=h: e.scalar_tensor_tensor(
                    sl(OFF_KV + _h * 128, 128), sl(OFF_KV + _h * 128, 128),
                    sl(C_BD + _h, 1), pb3[:, 0:128], op0=AL.mult, op1=AL.add))
                bank_free[3] = ("vV", kvu)

            for st in range(4):
                proj_stage(0, st)
            for h in range(H):
                for blk in range(NBLK):
                    if h + 1 < H:
                        if SPARSE and h == 0 and blk == 0:
                            r.wait("scalar", "dX", shared["scat_end"])
                            r.wait("vector", "dX", shared["scat_end"])
                        proj_stage(h + 1, blk)
                    scan_block(h, blk)

            st = sl(P2_ST, 16)
            for m in range(NM):
                for c in range(4):
                    r.v(lambda e, _m=m, _c=c: e.tensor_mul(
                        sl(P2_SB, 512), sl(P2_O + _m * 2048 + _c * 512, 512),
                        sl(P2_O + _m * 2048 + _c * 512, 512)))
                    r.v(lambda e, _c=c: e.reduce_sum(st[:, 3:4], sl(P2_SB, 512), axis=AX.X))
                    if c == 0:
                        r.v(lambda e, _m=m: e.tensor_copy(sl(P2_SSQ + _m, 1), st[:, 3:4]))
                    else:
                        r.v(lambda e, _m=m: e.tensor_add(
                            sl(P2_SSQ + _m, 1), sl(P2_SSQ + _m, 1), st[:, 3:4]))
                r.v(lambda e, _m=m: e.tensor_scalar(
                    st[:, 1:2], sl(P2_SSQ + _m, 1), 1.0 / DM, RMS_EPS,
                    op0=AL.mult, op1=AL.add))
                r.wait("scalar", "vV")
                r.a(lambda e: e.activation(st[:, 2:3], st[:, 1:2], AF.Sqrt))
                r.wait("vector", "aA")
                r.v(lambda e, _m=m: e.reciprocal(sl(P2_RMS + _m, 1), st[:, 2:3]))
            ew_free = [("tM", r.cnt["tM"]), ("tM", r.cnt["tM"])]
            sb_spill = [0, 0]

            def load_ew(slot, dram, col0):
                woff = (P2_EW0, P2_EW1)[slot]
                if ew_free[slot] is not None:
                    r.wait("sync", *ew_free[slot])
                cnt = r.dW(lambda e, _w=woff, _d=dram, _c=col0: e.dma_start(
                    out=sl(_w, 4096).rearrange("p (k n) -> p k n", n=256),
                    in_=wchunk_src(_d, _c, 256)))
                return woff, cnt

            for uc in range(8):
                wu, cu = load_ew(uc % 2, wqkvuT_d, 3 * DM + uc * 256)
                r.wait("tensor", "dW", cu)
                for m in range(NM):
                    b = uc % 2
                    bank_wait("tensor", b)
                    for k in range(16):
                        r.t(lambda e, _k=k, _m=m, _w=wu, _b=b: e.matmul(
                            PB[_b][:, 0:256],
                            sl(OFF_XNT + _k * TPC + _m * 128, 128),
                            sl(_w + _k * 256, 256),
                            start=(_k == 0), stop=(_k == 15)), inc=(k == 15))
                    ud = r.cnt["tM"]
                    r.wait("vector", "tM", ud)
                    dst = P2_O + m * 2048 + uc * 256
                    ev = r.v(lambda e, _d=dst, _b=b: e.tensor_mul(
                        sl(_d, 256), sl(_d, 256), PB[_b][:, 0:256]))
                    bank_free[b] = ("vV", ev)
                ew_free[uc % 2] = ("tM", r.cnt["tM"])
            if SPARSE:
                r.wait("vector", "dX", shared["scat_end"])
            for m in range(NM):
                ou_done = r.cnt["vV"]
                r.wait("tensor", "vV", ou_done)
                for grp in range(4):
                    b = 6 + grp % 2
                    bank_wait("tensor", b)
                    for f4 in range(4):
                        f = grp * 4 + f4
                        r.t(lambda e, _f=f, _f4=f4, _m=m, _b=b: e.transpose(
                            PB[_b][:, _f4 * 128:(_f4 + 1) * 128],
                            sl(P2_O + _m * 2048 + _f * 128, 128), sl(C_IDENT, 128)),
                            inc=(f4 == 3))
                    tdone = r.cnt["tM"]
                    r.wait("vector", "tM", tdone)
                    ev = r.v(lambda e, _g=grp, _b=b: e.tensor_copy(
                        sl(P2_OUT + _g * 512, 512), PB[_b][:, 0:512]))
                    bank_free[b] = ("vV", ev)
                out_done = r.cnt["vV"]
                for ch in range(8):
                    ww, cw = load_ew(ch % 2, woutT_d, ch * 256)
                    r.wait("tensor", "dW", cw)
                    r.wait("tensor", "vV", out_done)
                    b = 4 + ch % 2
                    bank_wait("tensor", b)
                    for k in range(16):
                        r.t(lambda e, _k=k, _w=ww, _b=b: e.matmul(
                            PB[_b][:, 0:256],
                            sl(P2_OUT + _k * 128, 128),
                            sl(_w + _k * 256, 256),
                            start=(_k == 0), stop=(_k == 15)), inc=(k == 15))
                    s_done = r.cnt["tM"]
                    ew_free[ch % 2] = ("tM", s_done)
                    r.wait("vector", "tM", s_done)
                    sb = P2_SB + (ch % 2) * 256
                    if sb_spill[ch % 2]:
                        r.wait("vector", "dX", sb_spill[ch % 2])
                    ev = r.v(lambda e, _b=b, _m=m, _s=sb: e.tensor_scalar(
                        sl(_s, 256), PB[_b][:, 0:256], sl(P2_RMS + _m, 1), None,
                        op0=AL.mult))
                    bank_free[b] = ("vV", ev)
                    r.wait("gpsimd", "vV", ev)
                    sb_spill[ch % 2] = r.dX(lambda e, _r=m * 128, _c=ch, _s=sb: e.dma_start(
                        out=s_spill_d[_r:_r + 128, _c * 256:(_c + 1) * 256],
                        in_=sl(_s, 256)))


        def emit_router():
            woff = T_WR
            r.dW(lambda e, _w=woff: e.dma_start(
                out=sl(_w, 16 * E).rearrange("p (k n) -> p k n", n=E),
                in_=wrT_d.rearrange("(k p) n -> k p n", p=128).transpose([1, 0, 2])))
            wr_done = r.cnt["dW"]
            r.wait("tensor", "dW", wr_done)
            for m in range(NM):
                b = 6 + m % 2
                bank_wait("tensor", b)
                for k in range(16):
                    r.t(lambda e, _k=k, _m=m, _w=woff, _b=b: e.matmul(
                        PB[_b][:, 0:E],
                        sl(OFF_XNT + _k * TPC + _m * 128, 128),
                        sl(_w + _k * E, E),
                        start=(_k == 0), stop=(_k == 15)), inc=(k == 15))
                lg_done = r.cnt["tM"]
                st = sl(P3_ST, 16)
                r.wait("vector", "tM", lg_done)
                r.wait("vector", "aA")
                r.v(lambda e, _b=b: e.reduce_max(st[:, 0:1], PB[_b][:, 0:E],
                                                 axis=AX.X, negate=True))
                r.wait("scalar", "vV")
                r.a(lambda e, _b=b: e.activation(sl(P3_LG, E), PB[_b][:, 0:E],
                                                 AF.Exp, bias=st[:, 0:1]))
                exp_done = r.cnt["aA"]
                bank_free[b] = ("aA", exp_done)
                r.wait("vector", "aA", exp_done)
                r.v(lambda e: e.reduce_sum(st[:, 1:2], sl(P3_LG, E), axis=AX.X))
                r.v(lambda e: e.reciprocal(st[:, 2:3], st[:, 1:2]))
                r.v(lambda e: e.tensor_scalar(sl(P3_LG, E), sl(P3_LG, E),
                                              st[:, 2:3], None, op0=AL.mult))
                r.v(lambda e: e.max(out=sl(P3_M8, 8), in_=sl(P3_LG, E)))
                r.v(lambda e: e.memset(sl(P3_M8 + 4, 4), -1.0))
                r.v(lambda e: e.match_replace(out=sl(P3_SC, E), in_to_replace=sl(P3_M8, 8),
                                              in_values=sl(P3_LG, E), imm_value=0.0))
                r.v(lambda e: e.tensor_sub(sl(P3_SC, E), sl(P3_LG, E), sl(P3_SC, E)))
                r.v(lambda e: e.reduce_sum(st[:, 3:4], sl(P3_SC, E), axis=AX.X))
                r.v(lambda e: e.reciprocal(st[:, 4:5], st[:, 3:4]))
                r.v(lambda e, _m=m: e.tensor_scalar(sl(OFF_WTOK + _m * E, E), sl(P3_SC, E),
                                                    st[:, 4:5], None, op0=AL.mult))

        def emit_tables_scatter():
            r.wait("tensor", "vV")
            for m in range(NM):
                b = 6 + m % 2
                bank_wait("tensor", b)
                r.t(lambda e, _m=m, _b=b: e.transpose(
                    PB[_b][0:32, 0:128], sl(OFF_WTOK + _m * E, E), sl(C_IDENT, 128)),
                    inc=True)
                td = r.cnt["tM"]
                r.wait("vector", "tM", td)
                ev = r.v(lambda e, _m=m, _b=b: e.tensor_copy(
                    A[0:32, T_WT + _m * 128:T_WT + (_m + 1) * 128], PB[_b][0:32, 0:128]))
                bank_free[b] = ("vV", ev)
            WT = A[0:32, T_WT:T_WT + TPC]
            MASK = A[0:32, T_MASK:T_MASK + TPC]
            r.v(lambda e: e.tensor_scalar(MASK, WT, 0.0, None, op0=AL.is_gt))
            r.v(lambda e: e.reduce_sum(A[0:32, T_CNT:T_CNT + 1], MASK, axis=AX.X))
            cur, nxt = T_CUMA, T_CUMB
            r.v(lambda e: e.tensor_copy(A[0:32, T_CUMA:T_CUMA + TPC], MASK))
            s = 1
            while s < TPC:
                r.v(lambda e, _c=cur, _n=nxt, _s=s: e.tensor_copy(
                    A[0:32, _n:_n + _s], A[0:32, _c:_c + _s]))
                r.v(lambda e, _c=cur, _n=nxt, _s=s: e.tensor_add(
                    A[0:32, _n + _s:_n + TPC], A[0:32, _c + _s:_c + TPC],
                    A[0:32, _c:_c + TPC - _s]))
                cur, nxt = nxt, cur
                s *= 2
            r.v(lambda e, _c=cur: e.tensor_scalar(
                A[0:32, T_POSM:T_POSM + TPC], A[0:32, _c:_c + TPC],
                A[0:32, T_CNT:T_CNT + 1], None, op0=AL.subtract))
            r.v(lambda e: e.tensor_scalar_mul(A[0:32, T_POSM:T_POSM + TPC],
                                              A[0:32, T_POSM:T_POSM + TPC], -1.0))
            r.v(lambda e: e.tensor_scalar_min(A[0:32, T_POSM:T_POSM + TPC],
                                              A[0:32, T_POSM:T_POSM + TPC], float(CAP - 1)))
            r.v(lambda e: e.tensor_scalar_mul(A[0:32, T_CNT + 1:T_CNT + 2],
                                              A[0:32, C_IOTP:C_IOTP + 1], float(CAP)))
            r.v(lambda e: e.tensor_scalar(A[0:32, T_POSM:T_POSM + TPC],
                                          A[0:32, T_POSM:T_POSM + TPC],
                                          A[0:32, T_CNT + 1:T_CNT + 2], 1.0,
                                          op0=AL.add, op1=AL.add))
            r.v(lambda e: e.tensor_mul(A[0:32, T_POSM:T_POSM + TPC],
                                       A[0:32, T_POSM:T_POSM + TPC], MASK))
            pm_done = r.cnt["vV"]
            r.wait("tensor", "vV", pm_done)
            for m in range(NM):
                b = 6 + m % 2
                bank_wait("tensor", b)
                r.t(lambda e, _m=m, _b=b: e.transpose(
                    PB[_b][:, 0:32], A[0:32, T_POSM + _m * 128:T_POSM + (_m + 1) * 128],
                    A[0:32, C_IDENT:C_IDENT + 32]), inc=True)
                td = r.cnt["tM"]
                r.wait("vector", "tM", td)
                ev = r.v(lambda e, _m=m, _b=b: e.tensor_copy(
                    sl(T_POSTOK + _m * E, E), PB[_b][:, 0:32]))
                bank_free[b] = ("vV", ev)
                r.v(lambda e, _m=m: e.max(out=sl(T_M8B, 8), in_=sl(T_POSTOK + _m * E, E)))
                r.v(lambda e, _m=m: e.tensor_scalar(
                    sl(OFF_P4T + _m * 4, 4), sl(T_M8B, 4), -1.0, None, op0=AL.add))
                for j in range(4):
                    r.v(lambda e, _m=m, _j=j: e.tensor_scalar(
                        sl(T_ONE, E), sl(T_POSTOK + _m * E, E),
                        sl(T_M8B + _j, 1), None, op0=AL.is_equal))
                    r.v(lambda e, _m=m: e.tensor_mul(
                        sl(T_ONE, E), sl(T_ONE, E), sl(OFF_WTOK + _m * E, E)))
                    r.v(lambda e, _m=m, _j=j: e.reduce_sum(
                        sl(OFF_P4T + 32 + _m * 4 + _j, 1), sl(T_ONE, E), axis=AX.X))
            r.v(lambda e: e.tensor_copy(IDXT[:, 0:32], sl(OFF_P4T, 32)))
            shared["cast_done"] = r.cnt["vV"]

        def emit_ffn_sparse():
            gu_free = [None, None]
            dw_free = [None, None]
            xs_state = {}

            def load_xs(e_idx):
                r.wait("sync", "dX", shared["scat_end"])
                r.dW(lambda e, _e=e_idx: e.dma_start(
                    out=sl(S4_XS, DM),
                    in_=xs_d[_e * CAP:_e * CAP + 128, :]))
                xs_state[e_idx] = r.dW(lambda e, _e=e_idx: e.dma_start(
                    out=A[0:64, S4_XS + DM:S4_XS + 2 * DM],
                    in_=xs_d[_e * CAP + 128:(_e + 1) * CAP, :]))

            def transposes(ei):
                xb = S4_XST + (ei % 2) * 3072
                r.wait("tensor", "dW", xs_state[ei])
                for grp in range(4):
                    b = 4 + grp % 2
                    bank_wait("tensor", b)
                    for q in range(4):
                        f = grp * 4 + q
                        r.t(lambda e, _f=f, _q=q, _b=b: e.transpose(
                            PB[_b][:, _q * 128:(_q + 1) * 128],
                            sl(S4_XS + _f * 128, 128), sl(C_IDENT, 128)),
                            inc=(q == 3))
                    td = r.cnt["tM"]
                    r.wait("vector", "tM", td)
                    ev = r.v(lambda e, _g=grp, _b=b, _x=xb: e.tensor_copy(
                        A[:, _x + _g * 4 * 192:_x + (_g * 4 + 4) * 192]
                        .rearrange("p (k n) -> p k n", n=192)[:, :, 0:128],
                        PB[_b].rearrange("p (k n) -> p k n", n=128)))
                    bank_free[b] = ("vV", ev)
                for grp in range(2):
                    b = 4 + grp % 2
                    bank_wait("tensor", b)
                    for q in range(8):
                        f = grp * 8 + q
                        r.t(lambda e, _f=f, _q=q, _b=b: e.transpose(
                            PB[_b][:, _q * 64:(_q + 1) * 64],
                            A[0:64, S4_XS + DM + _f * 128:S4_XS + DM + (_f + 1) * 128],
                            A[0:64, C_IDENT:C_IDENT + 64]),
                            inc=(q == 7))
                    td = r.cnt["tM"]
                    r.wait("vector", "tM", td)
                    ev = r.v(lambda e, _g=grp, _b=b, _x=xb: e.tensor_copy(
                        A[:, _x + 128 + _g * 8 * 192:_x + 128 + (_g * 8 + 8) * 192]
                        .rearrange("p (k n) -> p k n", n=192)[:, :, 0:64],
                        PB[_b].rearrange("p (k n) -> p k n", n=64)))
                    bank_free[b] = ("vV", ev)

            def phase_a(ei):
                xb = S4_XST + (ei % 2) * 3072
                hb = S4_HT + (ei % 2) * 1152
                xst_done = r.cnt["vV"]
                for ic in range(NIC):
                    gsl = ic % 2
                    gu_off = (P4_GU0, P4_GU1)[gsl]
                    if gu_free[gsl] is not None:
                        r.wait("sync", *gu_free[gsl])
                    cgu = r.dW(lambda e, _e=ei, _ic=ic, _g=gu_off: e.dma_start(
                        out=sl(_g, 4096).rearrange("p (k n) -> p k n", n=256),
                        in_=wguT_d[_e, :, _ic * 256:(_ic + 1) * 256]
                        .rearrange("(k p) n -> k p n", p=128).transpose([1, 0, 2])))
                    r.wait("tensor", "dW", cgu)
                    r.wait("tensor", "vV", xst_done)
                    bank_wait("tensor", 0)
                    bank_wait("tensor", 1)
                    for k in range(16):
                        r.t(lambda e, _k=k, _g=gu_off, _x=xb: e.matmul(
                            pb0[:, 0:CAP],
                            sl(_g + _k * 256, 128),
                            sl(_x + _k * 192, 192),
                            start=(_k == 0), stop=(_k == 15)), inc=(k == 15))
                    g_done = r.cnt["tM"]
                    for k in range(16):
                        r.t(lambda e, _k=k, _g=gu_off, _x=xb: e.matmul(
                            pb1[:, 0:CAP],
                            sl(_g + _k * 256 + 128, 128),
                            sl(_x + _k * 192, 192),
                            start=(_k == 0), stop=(_k == 15)), inc=(k == 15))
                    u_done = r.cnt["tM"]
                    gu_free[gsl] = ("tM", u_done)
                    r.wait("scalar", "tM", g_done)
                    sg = r.a(lambda e: e.activation(sl(S4_GS, CAP), pb0[:, 0:CAP], AF.Silu))
                    bank_free[0] = ("aA", sg)
                    r.wait("vector", "aA", sg)
                    r.wait("vector", "tM", u_done)
                    ht = r.v(lambda e, _ic=ic, _h=hb: e.tensor_mul(
                        sl(_h + _ic * CAP, CAP), sl(S4_GS, CAP), pb1[:, 0:CAP]))
                    bank_free[1] = ("vV", ht)

            def phase_b(ei):
                hb = S4_HT + (ei % 2) * 1152
                ht_all = r.cnt["vV"]
                r.wait("tensor", "vV", ht_all)
                YB = ((2, 3, 4, 5), (6, 7, 0, 1))
                for m in range(2):
                    for b in YB[m]:
                        bank_wait("tensor", b)
                for ic in range(NIC):
                    dsl = ic % 2
                    dw_off = (P4_DW0, P4_DW1)[dsl]
                    if dw_free[dsl] is not None:
                        r.wait("sync", *dw_free[dsl])
                    cdw = r.dW(lambda e, _e=ei, _ic=ic, _d=dw_off: e.dma_start(
                        out=sl(_d, DM), in_=wdwT_d[_e, _ic * 128:(_ic + 1) * 128, :]))
                    r.wait("tensor", "dW", cdw)
                    for m in range(2):
                        rows = 128 if m == 0 else CAP - 128
                        for nf in range(4):
                            r.t(lambda e, _m=m, _ic=ic, _nf=nf, _d=dw_off, _rw=rows,
                                _h=hb: e.matmul(
                                PB[YB[_m][_nf]][0:_rw, 0:512],
                                sl(_h + _ic * CAP + _m * 128, _rw),
                                sl(_d + _nf * 512, 512),
                                start=(_ic == 0), stop=(_ic == NIC - 1)),
                                inc=(m == 1 and nf == 3))
                    dw_free[dsl] = ("tM", r.cnt["tM"])
                y_done = r.cnt["tM"]
                for m in range(2):
                    rows = 128 if m == 0 else CAP - 128
                    r.wait("vector", "tM", y_done)
                    r.wait("vector", "dX")
                    ev = None
                    for nf in range(4):
                        ev = r.v(lambda e, _nf=nf, _m=m, _rw=rows: e.tensor_copy(
                            A[0:_rw, S4_YT + _nf * 512:S4_YT + (_nf + 1) * 512],
                            PB[YB[_m][_nf]][0:_rw, 0:512]))
                    for b in YB[m]:
                        bank_free[b] = ("vV", ev)
                    r.wait("gpsimd", "vV", ev)
                    r.dX(lambda e, _e=ei, _m=m, _rw=rows: e.dma_start(
                        out=ys_d[_e * CAP + _m * 128:_e * CAP + _m * 128 + _rw, :],
                        in_=A[0:_rw, S4_YT:S4_YT + DM]))

            load_xs(0)
            transposes(0)
            r.wait("sync", "tM", r.cnt["tM"])
            load_xs(1)
            for ei in range(E):
                phase_a(ei)
                if ei + 1 < E:
                    transposes(ei + 1)
                    if ei + 2 < E:
                        r.wait("sync", "tM", r.cnt["tM"])
                        load_xs(ei + 2)
                phase_b(ei)

            for m in range(NM):
                gb = (S4_GB0, S4_GB1)
                for j in range(4):
                    gbo = gb[j % 2]
                    r.wait("gpsimd", "vV")
                    gj = r.dX(lambda e, _m=m, _j=j, _g=gbo: e.indirect_dma_start(
                        out=sl(_g, DM), out_offset=None,
                        in_=ys_d,
                        in_offset=bass.IndirectOffsetOnAxis(IDXT[:, _m * 4 + _j:_m * 4 + _j + 1], 0)))
                    r.wait("vector", "dX", gj)
                    if j == 0:
                        r.v(lambda e, _m=m, _g=gbo: e.tensor_scalar(
                            sl(S4_CB, DM), sl(_g, DM),
                            sl(OFF_P4T + 32 + _m * 4, 1), None, op0=AL.mult))
                    else:
                        r.v(lambda e, _m=m, _j=j, _g=gbo: e.scalar_tensor_tensor(
                            sl(S4_CB, DM), sl(_g, DM),
                            sl(OFF_P4T + 32 + _m * 4 + _j, 1),
                            sl(S4_CB, DM), op0=AL.mult, op1=AL.add))
                moe_done = r.cnt["vV"]
                r.wait("scalar", "vV", moe_done)
                r.a(lambda e: e.activation(sl(S4_CB, DM), sl(S4_CB, DM), AF.Silu))
                g_done = r.cnt["aA"]
                row0 = m * 128
                r.dX(lambda e, _r=row0: e.dma_start(
                    out=sl(S4_CX, DM), in_=x_own_d[_r:_r + 128, :]))
                cs = r.dX(lambda e, _r=row0: e.dma_start(
                    out=sl(S4_CS, DM), in_=s_spill_d[_r:_r + 128, :]))
                r.wait("vector", "aA", g_done)
                r.wait("vector", "dX", cs)
                r.v(lambda e: e.tensor_mul(sl(S4_CB, DM), sl(S4_CB, DM), sl(S4_CS, DM)))
                r.v(lambda e: e.tensor_add(sl(S4_CB, DM), sl(S4_CB, DM), sl(S4_CX, DM)))
                o_done = r.cnt["vV"]
                r.wait("gpsimd", "vV", o_done)
                r.dX(lambda e, _r=row0: e.dma_start(
                    out=out_d[_r:_r + 128, :], in_=sl(S4_CB, DM)))
                r.wait("vector", "dX")

        def emit_ffn():
            gu_free = [None, None]
            dw_free = [None, None]
            for tc in range(2):
                tok0 = tc * 512
                r.wait("vector", "dX")
                for ei in range(E):
                    for ic in range(NIC):
                        gsl = ic % 2
                        gu_off = (P4_GU0, P4_GU1)[gsl]
                        if gu_free[gsl] is not None:
                            r.wait("sync", *gu_free[gsl])
                        cgu = r.dW(lambda e, _e=ei, _ic=ic, _g=gu_off: e.dma_start(
                            out=sl(_g, 4096).rearrange("p (k n) -> p k n", n=256),
                            in_=wguT_d[_e, :, _ic * 256:(_ic + 1) * 256]
                            .rearrange("(k p) n -> k p n", p=128).transpose([1, 0, 2])))
                        dw_off = (P4_DW0, P4_DW1)[gsl]
                        if dw_free[gsl] is not None:
                            r.wait("sync", *dw_free[gsl])
                        cdw = r.dW(lambda e, _e=ei, _ic=ic, _d=dw_off: e.dma_start(
                            out=sl(_d, DM), in_=wdwT_d[_e, _ic * 128:(_ic + 1) * 128, :]))
                        r.wait("tensor", "dW", cgu)
                        bank_wait("tensor", 0)
                        bank_wait("tensor", 1)
                        for k in range(16):
                            r.t(lambda e, _k=k, _g=gu_off, _t=tok0: e.matmul(
                                pb0[:, 0:512],
                                sl(_g + _k * 256, 128),
                                sl(OFF_XNT + _k * TPC + _t, 512),
                                start=(_k == 0), stop=(_k == 15)), inc=(k == 15))
                        g_done = r.cnt["tM"]
                        for k in range(16):
                            r.t(lambda e, _k=k, _g=gu_off, _t=tok0: e.matmul(
                                pb1[:, 0:512],
                                sl(_g + _k * 256 + 128, 128),
                                sl(OFF_XNT + _k * TPC + _t, 512),
                                start=(_k == 0), stop=(_k == 15)), inc=(k == 15))
                        u_done = r.cnt["tM"]
                        gu_free[gsl] = ("tM", u_done)
                        r.wait("scalar", "tM", g_done)
                        sg = r.a(lambda e: e.activation(sl(P4_GS, 512), pb0[:, 0:512], AF.Silu))
                        bank_free[0] = ("aA", sg)
                        r.wait("vector", "aA", sg)
                        r.wait("vector", "tM", u_done)
                        hdone = r.v(lambda e: e.tensor_mul(sl(P4_H, 512), sl(P4_GS, 512),
                                                           pb1[:, 0:512]))
                        bank_free[1] = ("vV", hdone)
                        r.wait("tensor", "vV", hdone)
                        r.wait("tensor", "dW", cdw)
                        ylast = None
                        for m in range(4):
                            yb = (2, 3, 4, 5) if m % 2 == 0 else (6, 7, 0, 1)
                            for bb in yb:
                                bank_wait("tensor", bb)
                            for nf in range(4):
                                r.t(lambda e, _m=m, _nf=nf, _d=dw_off, _b=yb[nf]: e.matmul(
                                    PB[_b][:, 0:512],
                                    sl(P4_H + _m * 128, 128),
                                    sl(_d + _nf * 512, 512),
                                    start=True, stop=True), inc=(nf == 3))
                            y_done = r.cnt["tM"]
                            dw_free[gsl] = ("tM", y_done)
                            r.wait("vector", "tM", y_done)
                            ev = None
                            for nf in range(4):
                                dst = P4_YAC + m * 2048 + nf * 512
                                wcol = OFF_WTOK + (tc * 4 + m) * E + ei
                                if ei == 0 and ic == 0:
                                    ev = r.v(lambda e, _d=dst, _b=yb[nf], _w=wcol: e.tensor_scalar(
                                        sl(_d, 512), PB[_b][:, 0:512], sl(_w, 1), None,
                                        op0=AL.mult))
                                else:
                                    ev = r.v(lambda e, _d=dst, _b=yb[nf], _w=wcol:
                                             e.scalar_tensor_tensor(
                                                 sl(_d, 512), PB[_b][:, 0:512], sl(_w, 1),
                                                 sl(_d, 512), op0=AL.mult, op1=AL.add))
                            for bb in yb:
                                bank_free[bb] = ("vV", ev)
                for m in range(4):
                    gm = r.cnt["vV"]
                    r.wait("scalar", "vV", gm)
                    r.a(lambda e, _m=m: e.activation(
                        sl(P4_YAC + _m * 2048, DM), sl(P4_YAC + _m * 2048, DM), AF.Silu))
                    g_done = r.cnt["aA"]
                    row0 = tok0 + m * 128
                    r.dX(lambda e, _r=row0: e.dma_start(
                        out=sl(P4_XTI, DM), in_=x_own_d[_r:_r + 128, :]))
                    cs = r.dX(lambda e, _r=row0: e.dma_start(
                        out=sl(P4_STI, DM), in_=s_spill_d[_r:_r + 128, :]))
                    r.wait("vector", "aA", g_done)
                    r.wait("vector", "dX", cs)
                    r.v(lambda e, _m=m: e.tensor_mul(
                        sl(P4_YAC + _m * 2048, DM), sl(P4_YAC + _m * 2048, DM), sl(P4_STI, DM)))
                    r.v(lambda e, _m=m: e.tensor_add(
                        sl(P4_YAC + _m * 2048, DM), sl(P4_YAC + _m * 2048, DM), sl(P4_XTI, DM)))
                    o_done = r.cnt["vV"]
                    r.wait("gpsimd", "vV", o_done)
                    r.dX(lambda e, _r=row0, _m=m: e.dma_start(
                        out=out_d[_r:_r + 128, :], in_=sl(P4_YAC + _m * 2048, DM)))

        emit_consts()
        r.barrier()
        for m in range(NM):
            emit_ln_tile(x_pre_d, OFF_XNTP, m, spill=False, bank_base=0)
        r.barrier()
        for h in range(H):
            emit_pre_head(h)
            if h % 2 == 1:
                emit_ln_tile(x_own_d, OFF_XNT, h // 2, spill=SPARSE, bank_base=6)
        r.barrier()
        if debug:
            r.dX(lambda e: e.dma_start(
                out=dbg["xnT_dump"].transpose([1, 0, 2]),
                in_=sl(OFF_XNT, 16 * TPC).rearrange("p (k n) -> p k n", n=TPC)))
            r.barrier()
        if SPARSE:
            emit_router()
            r.barrier()
            if debug:
                r.dX(lambda e: e.dma_start(
                    out=dbg["wtok_dump"].transpose([1, 0, 2]),
                    in_=sl(OFF_WTOK, NM * E).rearrange("p (m n) -> p m n", n=E)))
                r.barrier()
            emit_tables_scatter()
            r.barrier()
        emit_attention()
        r.barrier()
        if debug:
            r.dX(lambda e: e.dma_start(out=dbg["kv_dump"][:], in_=sl(OFF_KV, SZ_KV)))
            r.dX(lambda e: e.dma_start(out=dbg["s_dump"][:], in_=s_spill_d[:]))
            r.barrier()
        if SPARSE:
            emit_ffn_sparse()
        else:
            emit_router()
            r.barrier()
            if debug:
                r.dX(lambda e: e.dma_start(
                    out=dbg["wtok_dump"].transpose([1, 0, 2]),
                    in_=sl(OFF_WTOK, NM * E).rearrange("p (m n) -> p m n", n=E)))
                r.barrier()
            emit_ffn()
        fin_dX = r.cnt["dX"]
        fin_dW = r.cnt["dW"]
        r.rec["gpsimd"].append(lambda EE, SS, _v=fin_dX: EE["gpsimd"].wait_ge(SS["dX"], _v))
        r.rec["sync"].append(lambda EE, SS, _v=fin_dW: EE["sync"].wait_ge(SS["dW"], _v))

        def runner(eng_name):
            def go(eng_obj):
                EE = {eng_name: eng_obj}
                for fn in r.rec[eng_name]:
                    fn(EE, SEMS)
            return go

        block.sync(runner("sync"))
        block.gpsimd(runner("gpsimd"))
        block.vector(runner("vector"))
        block.scalar(runner("scalar"))
        block.tensor(runner("tensor"))

    return nc


def _prep_inputs(L0_attn_x, L0_attn_slope_rate, L0_attn_W_qkvu, L0_attn_W_out,
                 L0_ffn_router_w, L0_ffn_gate_w, L0_ffn_up_w, L0_ffn_down_w):
    x = np.ascontiguousarray(np.asarray(L0_attn_x, np.float32))
    slope = np.ascontiguousarray(np.asarray(L0_attn_slope_rate, np.float32).reshape(H))
    wqkvuT = np.ascontiguousarray(np.asarray(L0_attn_W_qkvu, np.float32).T)
    woutT = np.ascontiguousarray(np.asarray(L0_attn_W_out, np.float32).T)
    wrT = np.ascontiguousarray(np.asarray(L0_ffn_router_w, np.float32).T)
    gate = np.asarray(L0_ffn_gate_w, np.float32)
    up = np.asarray(L0_ffn_up_w, np.float32)
    down = np.asarray(L0_ffn_down_w, np.float32)
    gT = gate.transpose(0, 2, 1).reshape(E, DM, NIC, 128)
    uT = up.transpose(0, 2, 1).reshape(E, DM, NIC, 128)
    wguT = np.ascontiguousarray(
        np.concatenate([gT[:, :, :, None, :], uT[:, :, :, None, :]], axis=3)
        .reshape(E, DM, 2 * I))
    wdwT = np.ascontiguousarray(down.transpose(0, 2, 1))
    zeros = np.zeros((TPC, DM), np.float32)
    in_maps = []
    for c in range(8):
        b, half = c // 2, c % 2
        x_own = np.ascontiguousarray(x[b, half * TPC:(half + 1) * TPC, :])
        x_pre = np.ascontiguousarray(x[b, 0:TPC, :]) if half == 1 else zeros
        in_maps.append(dict(x_own=x_own, x_pre=x_pre, slope=slope, wqkvuT=wqkvuT,
                            woutT=woutT, wrT=wrT, wguT=wguT, wdwT=wdwT))
    return in_maps


def kernel(L0_attn_x, L0_attn_slope_rate, L0_attn_W_qkvu, L0_attn_W_out,
           L0_ffn_router_w, L0_ffn_gate_w, L0_ffn_up_w, L0_ffn_down_w):
    in_maps = _prep_inputs(L0_attn_x, L0_attn_slope_rate, L0_attn_W_qkvu, L0_attn_W_out,
                           L0_ffn_router_w, L0_ffn_gate_w, L0_ffn_up_w, L0_ffn_down_w)
    nc = build_nc(debug=False)
    res = run_bass_kernel_spmd(nc, in_maps, core_ids=list(range(8)))
    out = np.empty((B, N, DM), np.float32)
    for c in range(8):
        b, half = c // 2, c % 2
        out[b, half * TPC:(half + 1) * TPC, :] = res.results[c]["out"]
    return out


# version 12 (final)
# speedup vs baseline: 1.1173x; 1.1088x over previous
import numpy as np
import concourse.bass as bass
import concourse.mybir as mybir
from concourse.bass_utils import run_bass_kernel_spmd

F32 = mybir.dt.float32
AF = mybir.ActivationFunctionType
AX = mybir.AxisListType
AL = mybir.AluOpType

B, N, DM = 4, 2048, 2048
H, DH = 16, 128
BLK = 256
TPC = 1024
NBLK = TPC // BLK
NM = TPC // 128
G4 = 4 * DM
E, I = 32, 768
NIC = I // 128
LN_EPS = 1e-5
RMS_EPS = 1e-6
SPARSE = True
CAP = 192

ENGS = ("sync", "gpsimd", "vector", "scalar", "tensor")
SEM_PROD = {"dW": "sync", "dX": "gpsimd", "tM": "tensor", "vV": "vector",
            "aA": "scalar", "gG": "gpsimd"}
ASYNC_SEMS = ("dW", "dX")


class Rec:

    def __init__(self):
        self.rec = {e: [] for e in ENGS}
        self.cnt = {s: 0 for s in SEM_PROD}
        self.waited = {}

    def wait(self, eng, sem, val=None, force=False):
        if val is None:
            val = self.cnt[sem]
        if val <= 0:
            return
        if SEM_PROD[sem] == eng and sem not in ASYNC_SEMS and not force:
            return
        key = (eng, sem)
        if self.waited.get(key, 0) >= val:
            return
        self.waited[key] = val
        self.rec[eng].append(lambda EE, SS, _e=eng, _s=sem, _v=val: EE[_e].wait_ge(SS[_s], _v))

    def ins(self, eng, fn, sem=None, d=1):
        if sem is not None:
            self.cnt[sem] += d
        def go(EE, SS, _fn=fn, _sem=sem, _d=d, _eng=eng):
            rr = _fn(EE[_eng])
            if _sem is not None:
                rr.then_inc(SS[_sem], _d)
        self.rec[eng].append(go)
        return self.cnt[sem] if sem is not None else None

    def v(self, fn):
        self.wait("vector", "vV", force=True)
        return self.ins("vector", fn, "vV")

    def a(self, fn):
        self.wait("scalar", "aA", force=True)
        return self.ins("scalar", fn, "aA")

    def g(self, fn):
        self.wait("gpsimd", "gG", force=True)
        return self.ins("gpsimd", fn, "gG")

    def t(self, fn, inc=False):
        return self.ins("tensor", fn, "tM" if inc else None)

    def dW(self, fn):
        self.wait("sync", "dW", force=True)
        return self.ins("sync", fn, "dW", 16)

    def dX(self, fn):
        self.wait("gpsimd", "dX", force=True)
        return self.ins("gpsimd", fn, "dX", 16)

    def barrier(self):
        for eng in ENGS:
            for sem in SEM_PROD:
                self.wait(eng, sem)


def ceil128(x):
    return (x + 127) // 128 * 128


def build_nc(debug=False):
    nc = bass.Bass("TRN2", target_bir_lowering=False)

    x_own_d = nc.dram_tensor("x_own", [TPC, DM], F32, kind="ExternalInput").ap()
    x_pre_d = nc.dram_tensor("x_pre", [TPC, DM], F32, kind="ExternalInput").ap()
    slope_d = nc.dram_tensor("slope", [H], F32, kind="ExternalInput").ap()
    wqkvuT_d = nc.dram_tensor("wqkvuT", [DM, G4], F32, kind="ExternalInput").ap()
    woutT_d = nc.dram_tensor("woutT", [DM, DM], F32, kind="ExternalInput").ap()
    wrT_d = nc.dram_tensor("wrT", [DM, E], F32, kind="ExternalInput").ap()
    wguT_d = nc.dram_tensor("wguT", [E, DM, 2 * I], F32, kind="ExternalInput").ap()
    wdwT_d = nc.dram_tensor("wdwT", [E, I, DM], F32, kind="ExternalInput").ap()
    out_d = nc.dram_tensor("out", [TPC, DM], F32, kind="ExternalOutput").ap()
    s_spill_d = nc.dram_tensor("s_spill", [TPC, DM], F32, kind="Internal").ap()
    xs_d = nc.dram_tensor("xs_slots", [E * CAP, DM], F32, kind="Internal").ap()
    ys_d = nc.dram_tensor("ys_slots", [E * CAP, DM], F32, kind="Internal").ap()
    xn_d = nc.dram_tensor("xn_rows", [TPC, DM], F32, kind="Internal").ap()
    dbg = {}
    if debug:
        for name, shape in (("xnT_dump", [16, 128, TPC]), ("kv_dump", [128, H * DH]),
                            ("s_dump", [TPC, DM]), ("wtok_dump", [NM, 128, E]),
                            ("kvblk_dump", [NBLK, 128, H * DH]),
                            ("p0dbg", [4, 128, 512])):
            dbg[name] = nc.dram_tensor(name, shape, F32, kind="ExternalOutput").ap()

    OFF_XNT = 0
    SZ_XNT = 16 * TPC
    OFF_KV = OFF_XNT + SZ_XNT
    SZ_KV = H * DH
    OFF_WTOK = OFF_KV + SZ_KV
    OFF_P4T = OFF_WTOK + NM * E
    OFF_CONST = OFF_P4T + 64
    C_S = OFF_CONST
    C_NS = C_S + 16
    C_BD = C_NS + 16
    C_QD = C_BD + 16
    C_KD = C_QD + 32
    C_IOTP = C_KD + 32
    C_BIAS = C_IOTP + 8
    C_IDX0 = C_BIAS + 8
    C_IDX1 = C_IDX0 + 256
    C_RIDX0 = C_IDX1 + 256
    C_RIDX1 = C_RIDX0 + 256
    C_MASK0 = C_RIDX1 + 256
    C_MASK1 = C_MASK0 + 256
    C_IDENT = C_MASK1 + 256
    OFF_REST = ceil128(C_IDENT + 128)
    R = OFF_REST
    OFF_XNTP = R
    P0_XT0 = R + 16384 + 6144
    P0_XN = P0_XT0 + DM
    P0_ST = P0_XN + DM
    P0_END = P0_ST + 128
    P1_WK = R + 16384
    P1_WV = P1_WK + 2048
    P1_KKD = P1_WV + 2048
    P1_VST = P1_KKD + 1024
    P1_END = P1_VST + 1024
    P2_W0 = R
    P2_W1 = P2_W0 + 2048
    P2_EW0 = R
    P2_EW1 = R + 4096
    P2_HB0 = R + 4096
    P2_HB1 = P2_HB0 + 5120
    HB_QT = 0
    HB_KT = 1024
    HB_KTOK = 2048
    HB_VTOK = 3072
    HB_VT = 4096
    HB_D = 4096
    HB_QKD = 4608
    P2_O = P2_HB1 + 5120
    P2_RMS = P2_O + 16384
    P2_SSQ = P2_RMS + 8
    P2_ST = P2_SSQ + 8
    P2_OUT = P2_HB1
    P2_SB = P2_HB1 + 2048
    P2_END = ceil128(P2_ST + 48)
    P35 = R + 16384
    T_WR = P35
    P3_LG = T_WR + 512
    P3_SC = P3_LG + 64
    P3_M8 = P3_SC + 64
    P3_ST = P3_M8 + 16
    T_WT = ceil128(P3_ST + 16)
    T_MASK = T_WT + 1024
    T_CUMA = T_MASK + 1024
    T_CUMB = T_CUMA + 1024
    T_POSM = T_CUMB + 1024
    T_CNT = T_POSM + 1024
    T_POSTOK = T_CNT + 128
    T_M8B = T_POSTOK + 256
    T_ONE = T_M8B + 8
    P3_END = T_ONE + 64
    P4_GU0 = R
    P4_GU1 = P4_GU0 + 4096
    P4_DW0 = P4_GU1 + 4096
    P4_DW1 = P4_DW0 + DM
    P4_H = P4_DW1 + DM
    P4_GS = P4_H + 512
    P4_YAC = P4_GS + 512
    P4_XTI = P4_YAC + 8192
    P4_STI = P4_XTI + DM
    P4_END = P4_STI + DM
    S4_XS = P4_DW1 + DM
    S4_XST = S4_XS + 4096
    S4_HT = S4_XST + 6144
    S4_GS = S4_HT + 2304
    S4_YT = S4_GS + 192
    S4_END = S4_YT + 2048
    S4_GB0 = S4_XS
    S4_GB1 = S4_GB0 + 2048
    S4_CB = S4_GB1 + 2048
    S4_CX = S4_CB + 2048
    S4_CS = S4_CX + 2048
    P4_END = max(P4_END, S4_END, S4_CS + 2048)

    ARENA = ceil128(max(P0_END, P1_END, P2_END, P3_END, P4_END))
    assert ARENA <= 53000, f"arena {ARENA} cols exceeds SBUF"

    r = Rec()

    with (
        nc.sbuf_tensor("A", [128, ARENA], F32) as A,
        nc.sbuf_tensor("IDXT", [128, 64], mybir.dt.int32) as IDXT,
        nc.psum_tensor("pb0", [128, 512], F32) as pb0,
        nc.psum_tensor("pb1", [128, 512], F32) as pb1,
        nc.psum_tensor("pb2", [128, 512], F32) as pb2,
        nc.psum_tensor("pb3", [128, 512], F32) as pb3,
        nc.psum_tensor("pb4", [128, 512], F32) as pb4,
        nc.psum_tensor("pb5", [128, 512], F32) as pb5,
        nc.psum_tensor("pb6", [128, 512], F32) as pb6,
        nc.psum_tensor("pb7", [128, 512], F32) as pb7,
        nc.semaphore("dW") as sem_dW,
        nc.semaphore("dX") as sem_dX,
        nc.semaphore("tM") as sem_tM,
        nc.semaphore("vV") as sem_vV,
        nc.semaphore("aA") as sem_aA,
        nc.semaphore("gG") as sem_gG,
        nc.Block() as block,
    ):
        PB = [pb0, pb1, pb2, pb3, pb4, pb5, pb6, pb7]
        SEMS = {"dW": sem_dW, "dX": sem_dX, "tM": sem_tM, "vV": sem_vV,
                "aA": sem_aA, "gG": sem_gG}

        def sl(off, n):
            return A[:, off:off + n]

        bank_free = [None] * 8
        shared = {}

        def bank_wait(eng, b):
            if bank_free[b] is not None:
                r.wait(eng, *bank_free[b])

        def wchunk_src(dram, col0, ncols):
            return (dram[:, col0:col0 + ncols]
                    .rearrange("(k p) n -> k p n", p=128).transpose([1, 0, 2]))

        def emit_consts():
            r.dW(lambda e: e.dma_start(out=sl(C_S, 16),
                                       in_=slope_d[None, :].broadcast_to([128, H])))
            r.wait("vector", "dW")
            r.v(lambda e: e.tensor_scalar_mul(sl(C_NS, 16), sl(C_S, 16), -1.0))
            r.wait("scalar", "vV")
            r.a(lambda e: e.activation(sl(C_BD, 16), sl(C_S, 16), AF.Exp, scale=-float(BLK)))
            r.g(lambda e: e.iota(sl(C_IOTP, 1), pattern=[[0, 1]], base=0,
                                 channel_multiplier=1, allow_small_or_imprecise_dtypes=True))
            r.g(lambda e: e.iota(sl(C_IDX0, 256), pattern=[[1, 256]], base=0,
                                 channel_multiplier=-1, allow_small_or_imprecise_dtypes=True))
            r.g(lambda e: e.iota(sl(C_IDX1, 256), pattern=[[1, 256]], base=-128,
                                 channel_multiplier=-1, allow_small_or_imprecise_dtypes=True))
            r.v(lambda e: e.memset(sl(C_MASK0, 512 + 128), 1.0))
            r.wait("gpsimd", "vV")
            r.g(lambda e: e.affine_select(sl(C_MASK0, 256), sl(C_MASK0, 256),
                                          pattern=[[1, 256]], base=0, channel_multiplier=-1,
                                          compare_op=AL.is_ge, fill=0.0))
            r.g(lambda e: e.affine_select(sl(C_MASK1, 256), sl(C_MASK1, 256),
                                          pattern=[[1, 256]], base=-128, channel_multiplier=-1,
                                          compare_op=AL.is_ge, fill=0.0))
            r.g(lambda e: e.affine_select(sl(C_IDENT, 128), sl(C_IDENT, 128),
                                          pattern=[[-1, 128]], base=0, channel_multiplier=1,
                                          compare_op=AL.is_equal, fill=0.0))
            r.wait("vector", "gG")
            r.v(lambda e: e.tensor_scalar_max(sl(C_RIDX0, 512), sl(C_IDX0, 512), 0.0))
            for h in range(H):
                for c in range(2):
                    r.wait("vector", "aA")
                    r.wait("vector", "gG")
                    r.v(lambda e, _h=h, _c=c: e.tensor_scalar_mul(
                        sl(C_BIAS, 1), sl(C_NS + _h, 1), float(128 * _c + 1)))
                    r.v(lambda e, _h=h, _c=c: e.tensor_scalar_mul(
                        sl(C_BIAS + 1, 1), sl(C_S + _h, 1), float(128 * _c - 255)))
                    r.wait("scalar", "vV")
                    r.wait("scalar", "gG")
                    r.a(lambda e, _h=h, _c=c: e.activation(
                        sl(C_QD + 2 * _h + _c, 1), sl(C_IOTP, 1), AF.Exp,
                        scale=sl(C_NS + _h, 1), bias=sl(C_BIAS, 1)))
                    r.a(lambda e, _h=h, _c=c: e.activation(
                        sl(C_KD + 2 * _h + _c, 1), sl(C_IOTP, 1), AF.Exp,
                        scale=sl(C_S + _h, 1), bias=sl(C_BIAS + 1, 1)))
            r.v(lambda e: e.memset(sl(OFF_KV, SZ_KV), 0.0))
            if SPARSE:
                zt = r.v(lambda e: e.memset(sl(P0_XN, DM), 0.0))
                r.wait("sync", "vV", zt)
                for i in range(E * CAP // 128):
                    r.dW(lambda e, _i=i: e.dma_start(
                        out=xs_d[_i * 128:(_i + 1) * 128, :], in_=sl(P0_XN, DM)))

        ln_state = {"buf_release": 0, "xn_spill": 0}

        def emit_ln_tile(src_d, xnt_off, m, spill, bank_base):
            st = sl(P0_ST, 8)
            if ln_state["buf_release"]:
                r.wait("gpsimd", "vV", ln_state["buf_release"])
            ld = r.dX(lambda e, _m=m: e.dma_start(
                out=sl(P0_XT0, DM), in_=src_d[_m * 128:(_m + 1) * 128, :]))
            r.wait("vector", "dX", ld)
            r.v(lambda e: e.reduce_sum(st[:, 0:1], sl(P0_XT0, DM), axis=AX.X))
            r.v(lambda e: e.tensor_scalar_mul(st[:, 0:1], st[:, 0:1], 1.0 / DM))
            r.wait("scalar", "dX", max(ld, ln_state["xn_spill"]))
            r.wait("scalar", "tM")
            r.a(lambda e: e.activation(sl(P0_XN, DM), sl(P0_XT0, DM), AF.Square))
            r.wait("vector", "aA")
            r.v(lambda e: e.reduce_sum(st[:, 1:2], sl(P0_XN, DM), axis=AX.X))
            r.v(lambda e: e.tensor_scalar(st[:, 1:2], st[:, 1:2], 1.0 / DM, LN_EPS,
                                          op0=AL.mult, op1=AL.add))
            r.v(lambda e: e.tensor_mul(st[:, 2:3], st[:, 0:1], st[:, 0:1]))
            r.v(lambda e: e.tensor_sub(st[:, 1:2], st[:, 1:2], st[:, 2:3]))
            r.wait("scalar", "vV")
            r.a(lambda e: e.activation(st[:, 3:4], st[:, 1:2], AF.Sqrt))
            r.wait("vector", "aA")
            r.v(lambda e: e.reciprocal(st[:, 4:5], st[:, 3:4]))
            r.v(lambda e: e.tensor_mul(st[:, 5:6], st[:, 0:1], st[:, 4:5]))
            r.v(lambda e: e.tensor_scalar(sl(P0_XN, DM), sl(P0_XT0, DM),
                                          st[:, 4:5], st[:, 5:6],
                                          op0=AL.mult, op1=AL.subtract))
            xn_done = r.cnt["vV"]
            ln_state["buf_release"] = xn_done
            if spill:
                r.wait("gpsimd", "vV", xn_done)
                ln_state["xn_spill"] = r.dX(lambda e, _m=m: e.dma_start(
                    out=xn_d[_m * 128:(_m + 1) * 128, :], in_=sl(P0_XN, DM)))
            r.wait("tensor", "vV", xn_done)
            r.wait("tensor", "gG")
            for grp in range(4):
                b = bank_base + grp % 2
                bank_wait("tensor", b)
                for f4 in range(4):
                    f = grp * 4 + f4
                    r.t(lambda e, _f=f, _f4=f4, _b=b: e.transpose(
                        PB[_b][:, _f4 * 128:(_f4 + 1) * 128],
                        sl(P0_XN + _f * 128, 128), sl(C_IDENT, 128)),
                        inc=(f4 == 3))
                tdone = r.cnt["tM"]
                r.wait("vector", "tM", tdone)
                d0 = xnt_off + (grp * 4) * TPC + m * 128
                ev = r.v(lambda e, _d=d0, _b=b: e.tensor_copy(
                    A[:, _d:_d + 4 * TPC].rearrange("p (k n) -> p k n", n=TPC)[:, :, 0:128],
                    PB[_b].rearrange("p (k n) -> p k n", n=128)))
                bank_free[b] = ("vV", ev)

        p1_state = {"w_release": 0}

        def emit_pre_head(h):
            if p1_state["w_release"]:
                r.wait("sync", "tM", p1_state["w_release"])
            r.dW(lambda e, _h=h: e.dma_start(
                out=sl(P1_WK, 2048).rearrange("p (k n) -> p k n", n=128),
                in_=wchunk_src(wqkvuT_d, DM + _h * DH, 128)))
            r.dW(lambda e, _h=h: e.dma_start(
                out=sl(P1_WV, 2048).rearrange("p (k n) -> p k n", n=128),
                in_=wchunk_src(wqkvuT_d, 2 * DM + _h * DH, 128)))
            w_done = r.cnt["dW"]
            r.wait("tensor", "dW", w_done)
            for (woff, store, is_k) in ((P1_WK, P1_KKD, True), (P1_WV, P1_VST, False)):
                for m in range(NM):
                    b = (0 if is_k else 2) + m % 2
                    bank_wait("tensor", b)
                    for k in range(16):
                        r.t(lambda e, _k=k, _m=m, _w=woff, _b=b: e.matmul(
                            PB[_b][:, 0:128],
                            sl(OFF_XNTP + _k * TPC + _m * 128, 128),
                            sl(_w + _k * 128, 128),
                            start=(_k == 0), stop=(_k == 15)), inc=(k == 15))
                    tdone = r.cnt["tM"]
                    if is_k:
                        r.wait("scalar", "tM", tdone)
                        sdone = r.a(lambda e, _m=m, _b=b: e.activation(
                            sl(P1_KKD + _m * 128, 128), PB[_b][:, 0:128], AF.Silu))
                        bank_free[b] = ("aA", sdone)
                        r.wait("vector", "aA", sdone)
                        r.v(lambda e, _m=m, _h=h: e.tensor_scalar(
                            sl(P1_KKD + _m * 128, 128), sl(P1_KKD + _m * 128, 128),
                            sl(C_KD + 2 * _h + (_m % 2), 1), None, op0=AL.mult))
                    else:
                        r.wait("vector", "tM", tdone)
                        vdone = r.v(lambda e, _m=m, _b=b: e.tensor_copy(
                            sl(P1_VST + _m * 128, 128), PB[_b][:, 0:128]))
                        bank_free[b] = ("vV", vdone)
            p1_state["w_release"] = r.cnt["tM"]
            proj_done = r.cnt["vV"]
            r.wait("tensor", "vV", proj_done)
            for blk in range(NBLK):
                b = 4 + blk % 2
                bank_wait("tensor", b)
                for mc in range(2):
                    m = blk * 2 + mc
                    r.t(lambda e, _m=m, _b=b: e.matmul(
                        PB[_b][:, 0:128],
                        sl(P1_KKD + _m * 128, 128),
                        sl(P1_VST + _m * 128, 128),
                        start=(_m % 2 == 0), stop=(_m % 2 == 1)), inc=(mc == 1))
                tdone = r.cnt["tM"]
                r.wait("vector", "tM", tdone)
                vdone = r.v(lambda e, _h=h, _b=b: e.scalar_tensor_tensor(
                    sl(OFF_KV + _h * 128, 128), sl(OFF_KV + _h * 128, 128),
                    sl(C_BD + _h, 1), PB[_b][:, 0:128], op0=AL.mult, op1=AL.add))
                bank_free[b] = ("vV", vdone)


        slot_free = [None, None]

        def load_w_chunk(slot_idx, dram, col0, ncols):
            woff = (P2_W0, P2_W1)[slot_idx]
            if slot_free[slot_idx] is not None:
                r.wait("sync", *slot_free[slot_idx])
            cnt = r.dW(lambda e, _w=woff, _d=dram, _c=col0, _n=ncols: e.dma_start(
                out=sl(_w, 16 * _n).rearrange("p (k n) -> p k n", n=_n),
                in_=wchunk_src(_d, _c, _n)))
            return woff, cnt

        def emit_attention():
            if SPARSE:
                r.wait("gpsimd", "vV", shared["cast_done"])
                for m in range(NM):
                    r.dX(lambda e, _m=m: e.dma_start(
                        out=sl(P2_OUT, DM), in_=xn_d[_m * 128:(_m + 1) * 128, :]))
                    for j in range(4):
                        r.dX(lambda e, _m=m, _j=j: e.indirect_dma_start(
                            out=xs_d,
                            out_offset=bass.IndirectOffsetOnAxis(
                                IDXT[:, _m * 4 + _j:_m * 4 + _j + 1], 0),
                            in_=sl(P2_OUT, DM), in_offset=None))
                shared["scat_end"] = r.cnt["dX"]

            def proj_stage(h, stage):
                hb = (P2_HB0, P2_HB1)[h % 2]
                if stage < 3:
                    col0, dst = ((h * DH, hb + HB_QT), (DM + h * DH, hb + HB_KT),
                                 (2 * DM + h * DH, hb + HB_VT))[stage]
                    wq, cq = load_w_chunk(stage % 2, wqkvuT_d, col0, DH)
                    r.wait("tensor", "dW", cq)
                    bank_wait("tensor", 4)
                    bank_wait("tensor", 5)
                    for k in range(16):
                        for tc in range(2):
                            r.t(lambda e, _k=k, _tc=tc, _w=wq: e.matmul(
                                PB[4 + _tc][:, 0:512],
                                sl(_w + _k * DH, DH),
                                sl(OFF_XNT + _k * TPC + _tc * 512, 512),
                                start=(_k == 0), stop=(_k == 15)),
                                inc=(k == 15 and tc == 1))
                    pj_done = r.cnt["tM"]
                    slot_free[stage % 2] = ("tM", pj_done)
                    if stage < 2:
                        r.wait("scalar", "tM", pj_done)
                        ev = None
                        for tc in range(2):
                            ev = r.a(lambda e, _tc=tc, _d=dst: e.activation(
                                sl(_d + _tc * 512, 512), PB[4 + _tc][:, 0:512], AF.Silu))
                        bank_free[4] = ("aA", ev)
                        bank_free[5] = ("aA", ev)
                    else:
                        r.wait("vector", "tM", pj_done)
                        ev = None
                        for tc in range(2):
                            ev = r.v(lambda e, _tc=tc, _d=dst: e.tensor_copy(
                                sl(_d + _tc * 512, 512), PB[4 + _tc][:, 0:512]))
                        bank_free[4] = ("vV", ev)
                        bank_free[5] = ("vV", ev)
                    if stage == 1:
                        r.wait("tensor", "aA")
                        for grp in range(2):
                            b = 6 + grp % 2
                            bank_wait("tensor", b)
                            for q4 in range(4):
                                m = grp * 4 + q4
                                r.t(lambda e, _m=m, _q=q4, _b=b, _s=hb + HB_KT: e.transpose(
                                    PB[_b][:, _q * 128:(_q + 1) * 128],
                                    sl(_s + _m * 128, 128), sl(C_IDENT, 128)),
                                    inc=(q4 == 3))
                            td = r.cnt["tM"]
                            r.wait("vector", "tM", td)
                            ev = r.v(lambda e, _d=hb + HB_KTOK, _g=grp, _b=b: e.tensor_copy(
                                sl(_d + _g * 512, 512), PB[_b][:, 0:512]))
                            bank_free[b] = ("vV", ev)
                        for mt in range(NM):
                            r.v(lambda e, _mt=mt, _h=h, _d=hb + HB_KTOK: e.tensor_scalar(
                                sl(_d + _mt * 128, 128), sl(_d + _mt * 128, 128),
                                sl(C_KD + 2 * _h + (_mt % 2), 1), None, op0=AL.mult))
                else:
                    r.wait("tensor", "vV")
                    for grp in range(2):
                        b = 6 + grp % 2
                        bank_wait("tensor", b)
                        for q4 in range(4):
                            m = grp * 4 + q4
                            r.t(lambda e, _m=m, _q=q4, _b=b, _s=hb + HB_VT: e.transpose(
                                PB[_b][:, _q * 128:(_q + 1) * 128],
                                sl(_s + _m * 128, 128), sl(C_IDENT, 128)),
                                inc=(q4 == 3))
                        td = r.cnt["tM"]
                        r.wait("vector", "tM", td)
                        ev = r.v(lambda e, _d=hb + HB_VTOK, _g=grp, _b=b: e.tensor_copy(
                            sl(_d + _g * 512, 512), PB[_b][:, 0:512]))
                        bank_free[b] = ("vV", ev)
                    r.wait("scalar", "vV")
                    for c in range(2):
                        r.a(lambda e, _c=c, _h=h, _d=hb + HB_D: e.activation(
                            sl(_d + _c * 256, 256), sl(C_RIDX0 + _c * 256, 256),
                            AF.Exp, scale=sl(C_NS + _h, 1)))
                    r.wait("vector", "aA")
                    r.v(lambda e, _d=hb + HB_D: e.tensor_mul(sl(_d, 512), sl(_d, 512),
                                                             sl(C_MASK0, 512)))

            def scan_block(h, blk):
                hb = (P2_HB0, P2_HB1)[h % 2]
                t0 = blk * BLK
                r.wait("tensor", "vV")
                bank_wait("tensor", 1)
                for sc in range(2):
                    r.t(lambda e, _sc=sc, _t=t0, _hb=hb: e.matmul(
                        pb1[:, _sc * 256:(_sc + 1) * 256],
                        sl(_hb + HB_KT + _t + _sc * 128, 128),
                        sl(_hb + HB_QT + _t, 256),
                        start=True, stop=True), inc=(sc == 1))
                qk_done = r.cnt["tM"]
                r.wait("vector", "tM", qk_done)
                qkd = r.v(lambda e, _hb=hb: e.tensor_mul(
                    sl(_hb + HB_QKD, 512), pb1[:, 0:512], sl(_hb + HB_D, 512)))
                bank_free[1] = ("vV", qkd)
                bank_wait("tensor", 0)
                for mc in range(2):
                    r.t(lambda e, _mc=mc, _h=h, _t=t0, _hb=hb: e.matmul(
                        pb0[:, _mc * 128:(_mc + 1) * 128],
                        sl(_hb + HB_QT + _t + _mc * 128, 128),
                        sl(OFF_KV + _h * 128, 128),
                        start=True, stop=True), inc=(mc == 1))
                bank_wait("tensor", 3)
                for sc in range(2):
                    r.t(lambda e, _sc=sc, _b=blk, _hb=hb: e.matmul(
                        pb3[:, 0:128],
                        sl(_hb + HB_KTOK + (_b * 2 + _sc) * 128, 128),
                        sl(_hb + HB_VTOK + (_b * 2 + _sc) * 128, 128),
                        start=(_sc == 0), stop=(_sc == 1)), inc=(sc == 1))
                psC_done = r.cnt["tM"]
                r.wait("tensor", "vV", qkd)
                bank_wait("tensor", 2)
                for mc in range(2):
                    for sc in range(2):
                        r.t(lambda e, _mc=mc, _sc=sc, _b=blk, _hb=hb: e.matmul(
                            pb2[:, _mc * 128:(_mc + 1) * 128],
                            sl(_hb + HB_QKD + _sc * 256 + _mc * 128, 128),
                            sl(_hb + HB_VTOK + (_b * 2 + _sc) * 128, 128),
                            start=(_sc == 0), stop=(_sc == 1)),
                            inc=(mc == 1 and sc == 1))
                psB_done = r.cnt["tM"]
                r.wait("vector", "tM", psB_done)
                ev = None
                for mc in range(2):
                    m = blk * 2 + mc
                    dst = P2_O + m * 2048 + h * 128
                    r.v(lambda e, _d=dst, _mc=mc, _h=h: e.tensor_scalar(
                        sl(_d, 128), pb0[:, _mc * 128:(_mc + 1) * 128],
                        sl(C_QD + 2 * _h + _mc, 1), None, op0=AL.mult))
                    ev = r.v(lambda e, _d=dst, _mc=mc: e.tensor_add(
                        sl(_d, 128), sl(_d, 128), pb2[:, _mc * 128:(_mc + 1) * 128]))
                bank_free[0] = ("vV", ev)
                bank_free[2] = ("vV", ev)
                r.wait("vector", "tM", psC_done)
                kvu = r.v(lambda e, _h=h: e.scalar_tensor_tensor(
                    sl(OFF_KV + _h * 128, 128), sl(OFF_KV + _h * 128, 128),
                    sl(C_BD + _h, 1), pb3[:, 0:128], op0=AL.mult, op1=AL.add))
                bank_free[3] = ("vV", kvu)

            for st in range(4):
                proj_stage(0, st)
            for h in range(H):
                for blk in range(NBLK):
                    if h + 1 < H:
                        if SPARSE and h == 0 and blk == 0:
                            r.wait("scalar", "dX", shared["scat_end"])
                            r.wait("vector", "dX", shared["scat_end"])
                        proj_stage(h + 1, blk)
                    scan_block(h, blk)

            st = sl(P2_ST, 16)
            for m in range(NM):
                for c in range(4):
                    r.v(lambda e, _m=m, _c=c: e.tensor_mul(
                        sl(P2_SB, 512), sl(P2_O + _m * 2048 + _c * 512, 512),
                        sl(P2_O + _m * 2048 + _c * 512, 512)))
                    r.v(lambda e, _c=c: e.reduce_sum(st[:, 3:4], sl(P2_SB, 512), axis=AX.X))
                    if c == 0:
                        r.v(lambda e, _m=m: e.tensor_copy(sl(P2_SSQ + _m, 1), st[:, 3:4]))
                    else:
                        r.v(lambda e, _m=m: e.tensor_add(
                            sl(P2_SSQ + _m, 1), sl(P2_SSQ + _m, 1), st[:, 3:4]))
                r.v(lambda e, _m=m: e.tensor_scalar(
                    st[:, 1:2], sl(P2_SSQ + _m, 1), 1.0 / DM, RMS_EPS,
                    op0=AL.mult, op1=AL.add))
                r.wait("scalar", "vV")
                r.a(lambda e: e.activation(st[:, 2:3], st[:, 1:2], AF.Sqrt))
                r.wait("vector", "aA")
                r.v(lambda e, _m=m: e.reciprocal(sl(P2_RMS + _m, 1), st[:, 2:3]))
            ew_free = [("tM", r.cnt["tM"]), ("tM", r.cnt["tM"])]
            sb_spill = [0, 0]

            def load_ew(slot, dram, col0):
                woff = (P2_EW0, P2_EW1)[slot]
                if ew_free[slot] is not None:
                    r.wait("sync", *ew_free[slot])
                cnt = r.dW(lambda e, _w=woff, _d=dram, _c=col0: e.dma_start(
                    out=sl(_w, 4096).rearrange("p (k n) -> p k n", n=256),
                    in_=wchunk_src(_d, _c, 256)))
                return woff, cnt

            for uc in range(8):
                wu, cu = load_ew(uc % 2, wqkvuT_d, 3 * DM + uc * 256)
                r.wait("tensor", "dW", cu)
                for m in range(NM):
                    b = uc % 2
                    bank_wait("tensor", b)
                    for k in range(16):
                        r.t(lambda e, _k=k, _m=m, _w=wu, _b=b: e.matmul(
                            PB[_b][:, 0:256],
                            sl(OFF_XNT + _k * TPC + _m * 128, 128),
                            sl(_w + _k * 256, 256),
                            start=(_k == 0), stop=(_k == 15)), inc=(k == 15))
                    ud = r.cnt["tM"]
                    r.wait("vector", "tM", ud)
                    dst = P2_O + m * 2048 + uc * 256
                    ev = r.v(lambda e, _d=dst, _b=b: e.tensor_mul(
                        sl(_d, 256), sl(_d, 256), PB[_b][:, 0:256]))
                    bank_free[b] = ("vV", ev)
                ew_free[uc % 2] = ("tM", r.cnt["tM"])
            if SPARSE:
                r.wait("vector", "dX", shared["scat_end"])
            for m in range(NM):
                ou_done = r.cnt["vV"]
                r.wait("tensor", "vV", ou_done)
                for grp in range(4):
                    b = 6 + grp % 2
                    bank_wait("tensor", b)
                    for f4 in range(4):
                        f = grp * 4 + f4
                        r.t(lambda e, _f=f, _f4=f4, _m=m, _b=b: e.transpose(
                            PB[_b][:, _f4 * 128:(_f4 + 1) * 128],
                            sl(P2_O + _m * 2048 + _f * 128, 128), sl(C_IDENT, 128)),
                            inc=(f4 == 3))
                    tdone = r.cnt["tM"]
                    r.wait("vector", "tM", tdone)
                    ev = r.v(lambda e, _g=grp, _b=b: e.tensor_copy(
                        sl(P2_OUT + _g * 512, 512), PB[_b][:, 0:512]))
                    bank_free[b] = ("vV", ev)
                out_done = r.cnt["vV"]
                for ch in range(8):
                    ww, cw = load_ew(ch % 2, woutT_d, ch * 256)
                    r.wait("tensor", "dW", cw)
                    r.wait("tensor", "vV", out_done)
                    b = 4 + ch % 2
                    bank_wait("tensor", b)
                    for k in range(16):
                        r.t(lambda e, _k=k, _w=ww, _b=b: e.matmul(
                            PB[_b][:, 0:256],
                            sl(P2_OUT + _k * 128, 128),
                            sl(_w + _k * 256, 256),
                            start=(_k == 0), stop=(_k == 15)), inc=(k == 15))
                    s_done = r.cnt["tM"]
                    ew_free[ch % 2] = ("tM", s_done)
                    r.wait("vector", "tM", s_done)
                    sb = P2_SB + (ch % 2) * 256
                    if sb_spill[ch % 2]:
                        r.wait("vector", "dX", sb_spill[ch % 2])
                    ev = r.v(lambda e, _b=b, _m=m, _s=sb: e.tensor_scalar(
                        sl(_s, 256), PB[_b][:, 0:256], sl(P2_RMS + _m, 1), None,
                        op0=AL.mult))
                    bank_free[b] = ("vV", ev)
                    r.wait("gpsimd", "vV", ev)
                    sb_spill[ch % 2] = r.dX(lambda e, _r=m * 128, _c=ch, _s=sb: e.dma_start(
                        out=s_spill_d[_r:_r + 128, _c * 256:(_c + 1) * 256],
                        in_=sl(_s, 256)))


        def emit_router():
            woff = T_WR
            r.dW(lambda e, _w=woff: e.dma_start(
                out=sl(_w, 16 * E).rearrange("p (k n) -> p k n", n=E),
                in_=wrT_d.rearrange("(k p) n -> k p n", p=128).transpose([1, 0, 2])))
            wr_done = r.cnt["dW"]
            r.wait("tensor", "dW", wr_done)
            for m in range(NM):
                b = 6 + m % 2
                bank_wait("tensor", b)
                for k in range(16):
                    r.t(lambda e, _k=k, _m=m, _w=woff, _b=b: e.matmul(
                        PB[_b][:, 0:E],
                        sl(OFF_XNT + _k * TPC + _m * 128, 128),
                        sl(_w + _k * E, E),
                        start=(_k == 0), stop=(_k == 15)), inc=(k == 15))
                lg_done = r.cnt["tM"]
                st = sl(P3_ST, 16)
                r.wait("vector", "tM", lg_done)
                r.wait("vector", "aA")
                r.v(lambda e, _b=b: e.reduce_max(st[:, 0:1], PB[_b][:, 0:E],
                                                 axis=AX.X, negate=True))
                r.wait("scalar", "vV")
                r.a(lambda e, _b=b: e.activation(sl(P3_LG, E), PB[_b][:, 0:E],
                                                 AF.Exp, bias=st[:, 0:1]))
                exp_done = r.cnt["aA"]
                bank_free[b] = ("aA", exp_done)
                r.wait("vector", "aA", exp_done)
                r.v(lambda e: e.reduce_sum(st[:, 1:2], sl(P3_LG, E), axis=AX.X))
                r.v(lambda e: e.reciprocal(st[:, 2:3], st[:, 1:2]))
                r.v(lambda e: e.tensor_scalar(sl(P3_LG, E), sl(P3_LG, E),
                                              st[:, 2:3], None, op0=AL.mult))
                r.v(lambda e: e.max(out=sl(P3_M8, 8), in_=sl(P3_LG, E)))
                r.v(lambda e: e.memset(sl(P3_M8 + 4, 4), -1.0))
                r.v(lambda e: e.match_replace(out=sl(P3_SC, E), in_to_replace=sl(P3_M8, 8),
                                              in_values=sl(P3_LG, E), imm_value=0.0))
                r.v(lambda e: e.tensor_sub(sl(P3_SC, E), sl(P3_LG, E), sl(P3_SC, E)))
                r.v(lambda e: e.reduce_sum(st[:, 3:4], sl(P3_SC, E), axis=AX.X))
                r.v(lambda e: e.reciprocal(st[:, 4:5], st[:, 3:4]))
                r.v(lambda e, _m=m: e.tensor_scalar(sl(OFF_WTOK + _m * E, E), sl(P3_SC, E),
                                                    st[:, 4:5], None, op0=AL.mult))

        def emit_tables_scatter():
            r.wait("tensor", "vV")
            for m in range(NM):
                b = 6 + m % 2
                bank_wait("tensor", b)
                r.t(lambda e, _m=m, _b=b: e.transpose(
                    PB[_b][0:32, 0:128], sl(OFF_WTOK + _m * E, E), sl(C_IDENT, 128)),
                    inc=True)
                td = r.cnt["tM"]
                r.wait("vector", "tM", td)
                ev = r.v(lambda e, _m=m, _b=b: e.tensor_copy(
                    A[0:32, T_WT + _m * 128:T_WT + (_m + 1) * 128], PB[_b][0:32, 0:128]))
                bank_free[b] = ("vV", ev)
            WT = A[0:32, T_WT:T_WT + TPC]
            MASK = A[0:32, T_MASK:T_MASK + TPC]
            r.v(lambda e: e.tensor_scalar(MASK, WT, 0.0, None, op0=AL.is_gt))
            r.v(lambda e: e.reduce_sum(A[0:32, T_CNT:T_CNT + 1], MASK, axis=AX.X))
            cur, nxt = T_CUMA, T_CUMB
            r.v(lambda e: e.tensor_copy(A[0:32, T_CUMA:T_CUMA + TPC], MASK))
            s = 1
            while s < TPC:
                r.v(lambda e, _c=cur, _n=nxt, _s=s: e.tensor_copy(
                    A[0:32, _n:_n + _s], A[0:32, _c:_c + _s]))
                r.v(lambda e, _c=cur, _n=nxt, _s=s: e.tensor_add(
                    A[0:32, _n + _s:_n + TPC], A[0:32, _c + _s:_c + TPC],
                    A[0:32, _c:_c + TPC - _s]))
                cur, nxt = nxt, cur
                s *= 2
            r.v(lambda e, _c=cur: e.tensor_scalar(
                A[0:32, T_POSM:T_POSM + TPC], A[0:32, _c:_c + TPC],
                A[0:32, T_CNT:T_CNT + 1], None, op0=AL.subtract))
            r.v(lambda e: e.tensor_scalar_mul(A[0:32, T_POSM:T_POSM + TPC],
                                              A[0:32, T_POSM:T_POSM + TPC], -1.0))
            r.v(lambda e: e.tensor_scalar_min(A[0:32, T_POSM:T_POSM + TPC],
                                              A[0:32, T_POSM:T_POSM + TPC], float(CAP - 1)))
            r.v(lambda e: e.tensor_scalar_mul(A[0:32, T_CNT + 1:T_CNT + 2],
                                              A[0:32, C_IOTP:C_IOTP + 1], float(CAP)))
            r.v(lambda e: e.tensor_scalar(A[0:32, T_POSM:T_POSM + TPC],
                                          A[0:32, T_POSM:T_POSM + TPC],
                                          A[0:32, T_CNT + 1:T_CNT + 2], 1.0,
                                          op0=AL.add, op1=AL.add))
            r.v(lambda e: e.tensor_mul(A[0:32, T_POSM:T_POSM + TPC],
                                       A[0:32, T_POSM:T_POSM + TPC], MASK))
            pm_done = r.cnt["vV"]
            r.wait("tensor", "vV", pm_done)
            for m in range(NM):
                b = 6 + m % 2
                bank_wait("tensor", b)
                r.t(lambda e, _m=m, _b=b: e.transpose(
                    PB[_b][:, 0:32], A[0:32, T_POSM + _m * 128:T_POSM + (_m + 1) * 128],
                    A[0:32, C_IDENT:C_IDENT + 32]), inc=True)
                td = r.cnt["tM"]
                r.wait("vector", "tM", td)
                ev = r.v(lambda e, _m=m, _b=b: e.tensor_copy(
                    sl(T_POSTOK + _m * E, E), PB[_b][:, 0:32]))
                bank_free[b] = ("vV", ev)
                r.v(lambda e, _m=m: e.max(out=sl(T_M8B, 8), in_=sl(T_POSTOK + _m * E, E)))
                r.v(lambda e, _m=m: e.tensor_scalar(
                    sl(OFF_P4T + _m * 4, 4), sl(T_M8B, 4), -1.0, None, op0=AL.add))
                for j in range(4):
                    r.v(lambda e, _m=m, _j=j: e.tensor_scalar(
                        sl(T_ONE, E), sl(T_POSTOK + _m * E, E),
                        sl(T_M8B + _j, 1), None, op0=AL.is_equal))
                    r.v(lambda e, _m=m: e.tensor_mul(
                        sl(T_ONE, E), sl(T_ONE, E), sl(OFF_WTOK + _m * E, E)))
                    r.v(lambda e, _m=m, _j=j: e.reduce_sum(
                        sl(OFF_P4T + 32 + _m * 4 + _j, 1), sl(T_ONE, E), axis=AX.X))
            r.v(lambda e: e.tensor_copy(IDXT[:, 0:32], sl(OFF_P4T, 32)))
            shared["cast_done"] = r.cnt["vV"]

        def emit_ffn_sparse():
            gu_free = [None, None]
            dw_free = [None, None]
            xs_state = {}

            def load_xs(e_idx):
                r.wait("sync", "dX", shared["scat_end"])
                r.dW(lambda e, _e=e_idx: e.dma_start(
                    out=sl(S4_XS, DM),
                    in_=xs_d[_e * CAP:_e * CAP + 128, :]))
                xs_state[e_idx] = r.dW(lambda e, _e=e_idx: e.dma_start(
                    out=A[0:64, S4_XS + DM:S4_XS + 2 * DM],
                    in_=xs_d[_e * CAP + 128:(_e + 1) * CAP, :]))

            def transposes(ei):
                xb = S4_XST + (ei % 2) * 3072
                r.wait("tensor", "dW", xs_state[ei])
                for grp in range(4):
                    b = 4 + grp % 2
                    bank_wait("tensor", b)
                    for q in range(4):
                        f = grp * 4 + q
                        r.t(lambda e, _f=f, _q=q, _b=b: e.transpose(
                            PB[_b][:, _q * 128:(_q + 1) * 128],
                            sl(S4_XS + _f * 128, 128), sl(C_IDENT, 128)),
                            inc=(q == 3))
                    td = r.cnt["tM"]
                    r.wait("vector", "tM", td)
                    ev = r.v(lambda e, _g=grp, _b=b, _x=xb: e.tensor_copy(
                        A[:, _x + _g * 4 * 192:_x + (_g * 4 + 4) * 192]
                        .rearrange("p (k n) -> p k n", n=192)[:, :, 0:128],
                        PB[_b].rearrange("p (k n) -> p k n", n=128)))
                    bank_free[b] = ("vV", ev)
                for grp in range(2):
                    b = 4 + grp % 2
                    bank_wait("tensor", b)
                    for q in range(8):
                        f = grp * 8 + q
                        r.t(lambda e, _f=f, _q=q, _b=b: e.transpose(
                            PB[_b][:, _q * 64:(_q + 1) * 64],
                            A[0:64, S4_XS + DM + _f * 128:S4_XS + DM + (_f + 1) * 128],
                            A[0:64, C_IDENT:C_IDENT + 64]),
                            inc=(q == 7))
                    td = r.cnt["tM"]
                    r.wait("vector", "tM", td)
                    ev = r.v(lambda e, _g=grp, _b=b, _x=xb: e.tensor_copy(
                        A[:, _x + 128 + _g * 8 * 192:_x + 128 + (_g * 8 + 8) * 192]
                        .rearrange("p (k n) -> p k n", n=192)[:, :, 0:64],
                        PB[_b].rearrange("p (k n) -> p k n", n=64)))
                    bank_free[b] = ("vV", ev)

            def phase_a(ei):
                xb = S4_XST + (ei % 2) * 3072
                hb = S4_HT + (ei % 2) * 1152
                xst_done = r.cnt["vV"]
                for ic in range(NIC):
                    gsl = ic % 2
                    gu_off = (P4_GU0, P4_GU1)[gsl]
                    if gu_free[gsl] is not None:
                        r.wait("sync", *gu_free[gsl])
                    cgu = r.dW(lambda e, _e=ei, _ic=ic, _g=gu_off: e.dma_start(
                        out=sl(_g, 4096).rearrange("p (k n) -> p k n", n=256),
                        in_=wguT_d[_e, :, _ic * 256:(_ic + 1) * 256]
                        .rearrange("(k p) n -> k p n", p=128).transpose([1, 0, 2])))
                    r.wait("tensor", "dW", cgu)
                    r.wait("tensor", "vV", xst_done)
                    bank_wait("tensor", 0)
                    bank_wait("tensor", 1)
                    for k in range(16):
                        r.t(lambda e, _k=k, _g=gu_off, _x=xb: e.matmul(
                            pb0[:, 0:CAP],
                            sl(_g + _k * 256, 128),
                            sl(_x + _k * 192, 192),
                            start=(_k == 0), stop=(_k == 15)), inc=(k == 15))
                    g_done = r.cnt["tM"]
                    for k in range(16):
                        r.t(lambda e, _k=k, _g=gu_off, _x=xb: e.matmul(
                            pb1[:, 0:CAP],
                            sl(_g + _k * 256 + 128, 128),
                            sl(_x + _k * 192, 192),
                            start=(_k == 0), stop=(_k == 15)), inc=(k == 15))
                    u_done = r.cnt["tM"]
                    gu_free[gsl] = ("tM", u_done)
                    r.wait("scalar", "tM", g_done)
                    sg = r.a(lambda e: e.activation(sl(S4_GS, CAP), pb0[:, 0:CAP], AF.Silu))
                    bank_free[0] = ("aA", sg)
                    r.wait("vector", "aA", sg)
                    r.wait("vector", "tM", u_done)
                    ht = r.v(lambda e, _ic=ic, _h=hb: e.tensor_mul(
                        sl(_h + _ic * CAP, CAP), sl(S4_GS, CAP), pb1[:, 0:CAP]))
                    bank_free[1] = ("vV", ht)

            def phase_b(ei):
                hb = S4_HT + (ei % 2) * 1152
                ht_all = r.cnt["vV"]
                r.wait("tensor", "vV", ht_all)
                YB = ((2, 3, 4, 5), (6, 7, 0, 1))
                for m in range(2):
                    for b in YB[m]:
                        bank_wait("tensor", b)
                for ic in range(NIC):
                    dsl = ic % 2
                    dw_off = (P4_DW0, P4_DW1)[dsl]
                    if dw_free[dsl] is not None:
                        r.wait("gpsimd", *dw_free[dsl])
                    cdw = r.dX(lambda e, _e=ei, _ic=ic, _d=dw_off: e.dma_start(
                        out=sl(_d, DM), in_=wdwT_d[_e, _ic * 128:(_ic + 1) * 128, :]))
                    r.wait("tensor", "dX", cdw)
                    for m in range(2):
                        rows = 128 if m == 0 else CAP - 128
                        for nf in range(4):
                            r.t(lambda e, _m=m, _ic=ic, _nf=nf, _d=dw_off, _rw=rows,
                                _h=hb: e.matmul(
                                PB[YB[_m][_nf]][0:_rw, 0:512],
                                sl(_h + _ic * CAP + _m * 128, _rw),
                                sl(_d + _nf * 512, 512),
                                start=(_ic == 0), stop=(_ic == NIC - 1)),
                                inc=(m == 1 and nf == 3))
                    dw_free[dsl] = ("tM", r.cnt["tM"])
                y_done = r.cnt["tM"]
                for m in range(2):
                    rows = 128 if m == 0 else CAP - 128
                    r.wait("vector", "tM", y_done)
                    r.wait("vector", "dX")
                    ev = None
                    for nf in range(4):
                        ev = r.v(lambda e, _nf=nf, _m=m, _rw=rows: e.tensor_copy(
                            A[0:_rw, S4_YT + _nf * 512:S4_YT + (_nf + 1) * 512],
                            PB[YB[_m][_nf]][0:_rw, 0:512]))
                    for b in YB[m]:
                        bank_free[b] = ("vV", ev)
                    r.wait("gpsimd", "vV", ev)
                    r.dX(lambda e, _e=ei, _m=m, _rw=rows: e.dma_start(
                        out=ys_d[_e * CAP + _m * 128:_e * CAP + _m * 128 + _rw, :],
                        in_=A[0:_rw, S4_YT:S4_YT + DM]))

            load_xs(0)
            transposes(0)
            r.wait("sync", "tM", r.cnt["tM"])
            load_xs(1)
            for ei in range(E):
                phase_a(ei)
                if ei + 1 < E:
                    transposes(ei + 1)
                    if ei + 2 < E:
                        r.wait("sync", "tM", r.cnt["tM"])
                        load_xs(ei + 2)
                phase_b(ei)

            for m in range(NM):
                gb = (S4_GB0, S4_GB1)
                for j in range(4):
                    gbo = gb[j % 2]
                    r.wait("gpsimd", "vV")
                    gj = r.dX(lambda e, _m=m, _j=j, _g=gbo: e.indirect_dma_start(
                        out=sl(_g, DM), out_offset=None,
                        in_=ys_d,
                        in_offset=bass.IndirectOffsetOnAxis(IDXT[:, _m * 4 + _j:_m * 4 + _j + 1], 0)))
                    r.wait("vector", "dX", gj)
                    if j == 0:
                        r.v(lambda e, _m=m, _g=gbo: e.tensor_scalar(
                            sl(S4_CB, DM), sl(_g, DM),
                            sl(OFF_P4T + 32 + _m * 4, 1), None, op0=AL.mult))
                    else:
                        r.v(lambda e, _m=m, _j=j, _g=gbo: e.scalar_tensor_tensor(
                            sl(S4_CB, DM), sl(_g, DM),
                            sl(OFF_P4T + 32 + _m * 4 + _j, 1),
                            sl(S4_CB, DM), op0=AL.mult, op1=AL.add))
                moe_done = r.cnt["vV"]
                r.wait("scalar", "vV", moe_done)
                r.a(lambda e: e.activation(sl(S4_CB, DM), sl(S4_CB, DM), AF.Silu))
                g_done = r.cnt["aA"]
                row0 = m * 128
                r.dX(lambda e, _r=row0: e.dma_start(
                    out=sl(S4_CX, DM), in_=x_own_d[_r:_r + 128, :]))
                cs = r.dX(lambda e, _r=row0: e.dma_start(
                    out=sl(S4_CS, DM), in_=s_spill_d[_r:_r + 128, :]))
                r.wait("vector", "aA", g_done)
                r.wait("vector", "dX", cs)
                r.v(lambda e: e.tensor_mul(sl(S4_CB, DM), sl(S4_CB, DM), sl(S4_CS, DM)))
                r.v(lambda e: e.tensor_add(sl(S4_CB, DM), sl(S4_CB, DM), sl(S4_CX, DM)))
                o_done = r.cnt["vV"]
                r.wait("gpsimd", "vV", o_done)
                r.dX(lambda e, _r=row0: e.dma_start(
                    out=out_d[_r:_r + 128, :], in_=sl(S4_CB, DM)))
                r.wait("vector", "dX")

        def emit_ffn():
            gu_free = [None, None]
            dw_free = [None, None]
            for tc in range(2):
                tok0 = tc * 512
                r.wait("vector", "dX")
                for ei in range(E):
                    for ic in range(NIC):
                        gsl = ic % 2
                        gu_off = (P4_GU0, P4_GU1)[gsl]
                        if gu_free[gsl] is not None:
                            r.wait("sync", *gu_free[gsl])
                        cgu = r.dW(lambda e, _e=ei, _ic=ic, _g=gu_off: e.dma_start(
                            out=sl(_g, 4096).rearrange("p (k n) -> p k n", n=256),
                            in_=wguT_d[_e, :, _ic * 256:(_ic + 1) * 256]
                            .rearrange("(k p) n -> k p n", p=128).transpose([1, 0, 2])))
                        dw_off = (P4_DW0, P4_DW1)[gsl]
                        if dw_free[gsl] is not None:
                            r.wait("sync", *dw_free[gsl])
                        cdw = r.dW(lambda e, _e=ei, _ic=ic, _d=dw_off: e.dma_start(
                            out=sl(_d, DM), in_=wdwT_d[_e, _ic * 128:(_ic + 1) * 128, :]))
                        r.wait("tensor", "dW", cgu)
                        bank_wait("tensor", 0)
                        bank_wait("tensor", 1)
                        for k in range(16):
                            r.t(lambda e, _k=k, _g=gu_off, _t=tok0: e.matmul(
                                pb0[:, 0:512],
                                sl(_g + _k * 256, 128),
                                sl(OFF_XNT + _k * TPC + _t, 512),
                                start=(_k == 0), stop=(_k == 15)), inc=(k == 15))
                        g_done = r.cnt["tM"]
                        for k in range(16):
                            r.t(lambda e, _k=k, _g=gu_off, _t=tok0: e.matmul(
                                pb1[:, 0:512],
                                sl(_g + _k * 256 + 128, 128),
                                sl(OFF_XNT + _k * TPC + _t, 512),
                                start=(_k == 0), stop=(_k == 15)), inc=(k == 15))
                        u_done = r.cnt["tM"]
                        gu_free[gsl] = ("tM", u_done)
                        r.wait("scalar", "tM", g_done)
                        sg = r.a(lambda e: e.activation(sl(P4_GS, 512), pb0[:, 0:512], AF.Silu))
                        bank_free[0] = ("aA", sg)
                        r.wait("vector", "aA", sg)
                        r.wait("vector", "tM", u_done)
                        hdone = r.v(lambda e: e.tensor_mul(sl(P4_H, 512), sl(P4_GS, 512),
                                                           pb1[:, 0:512]))
                        bank_free[1] = ("vV", hdone)
                        r.wait("tensor", "vV", hdone)
                        r.wait("tensor", "dW", cdw)
                        ylast = None
                        for m in range(4):
                            yb = (2, 3, 4, 5) if m % 2 == 0 else (6, 7, 0, 1)
                            for bb in yb:
                                bank_wait("tensor", bb)
                            for nf in range(4):
                                r.t(lambda e, _m=m, _nf=nf, _d=dw_off, _b=yb[nf]: e.matmul(
                                    PB[_b][:, 0:512],
                                    sl(P4_H + _m * 128, 128),
                                    sl(_d + _nf * 512, 512),
                                    start=True, stop=True), inc=(nf == 3))
                            y_done = r.cnt["tM"]
                            dw_free[gsl] = ("tM", y_done)
                            r.wait("vector", "tM", y_done)
                            ev = None
                            for nf in range(4):
                                dst = P4_YAC + m * 2048 + nf * 512
                                wcol = OFF_WTOK + (tc * 4 + m) * E + ei
                                if ei == 0 and ic == 0:
                                    ev = r.v(lambda e, _d=dst, _b=yb[nf], _w=wcol: e.tensor_scalar(
                                        sl(_d, 512), PB[_b][:, 0:512], sl(_w, 1), None,
                                        op0=AL.mult))
                                else:
                                    ev = r.v(lambda e, _d=dst, _b=yb[nf], _w=wcol:
                                             e.scalar_tensor_tensor(
                                                 sl(_d, 512), PB[_b][:, 0:512], sl(_w, 1),
                                                 sl(_d, 512), op0=AL.mult, op1=AL.add))
                            for bb in yb:
                                bank_free[bb] = ("vV", ev)
                for m in range(4):
                    gm = r.cnt["vV"]
                    r.wait("scalar", "vV", gm)
                    r.a(lambda e, _m=m: e.activation(
                        sl(P4_YAC + _m * 2048, DM), sl(P4_YAC + _m * 2048, DM), AF.Silu))
                    g_done = r.cnt["aA"]
                    row0 = tok0 + m * 128
                    r.dX(lambda e, _r=row0: e.dma_start(
                        out=sl(P4_XTI, DM), in_=x_own_d[_r:_r + 128, :]))
                    cs = r.dX(lambda e, _r=row0: e.dma_start(
                        out=sl(P4_STI, DM), in_=s_spill_d[_r:_r + 128, :]))
                    r.wait("vector", "aA", g_done)
                    r.wait("vector", "dX", cs)
                    r.v(lambda e, _m=m: e.tensor_mul(
                        sl(P4_YAC + _m * 2048, DM), sl(P4_YAC + _m * 2048, DM), sl(P4_STI, DM)))
                    r.v(lambda e, _m=m: e.tensor_add(
                        sl(P4_YAC + _m * 2048, DM), sl(P4_YAC + _m * 2048, DM), sl(P4_XTI, DM)))
                    o_done = r.cnt["vV"]
                    r.wait("gpsimd", "vV", o_done)
                    r.dX(lambda e, _r=row0, _m=m: e.dma_start(
                        out=out_d[_r:_r + 128, :], in_=sl(P4_YAC + _m * 2048, DM)))

        emit_consts()
        r.barrier()
        for m in range(NM):
            emit_ln_tile(x_pre_d, OFF_XNTP, m, spill=False, bank_base=0)
        r.barrier()
        for h in range(H):
            emit_pre_head(h)
            if h % 2 == 1:
                emit_ln_tile(x_own_d, OFF_XNT, h // 2, spill=SPARSE, bank_base=6)
        r.barrier()
        if debug:
            r.dX(lambda e: e.dma_start(
                out=dbg["xnT_dump"].transpose([1, 0, 2]),
                in_=sl(OFF_XNT, 16 * TPC).rearrange("p (k n) -> p k n", n=TPC)))
            r.barrier()
        if SPARSE:
            emit_router()
            r.barrier()
            if debug:
                r.dX(lambda e: e.dma_start(
                    out=dbg["wtok_dump"].transpose([1, 0, 2]),
                    in_=sl(OFF_WTOK, NM * E).rearrange("p (m n) -> p m n", n=E)))
                r.barrier()
            emit_tables_scatter()
            r.barrier()
        emit_attention()
        r.barrier()
        if debug:
            r.dX(lambda e: e.dma_start(out=dbg["kv_dump"][:], in_=sl(OFF_KV, SZ_KV)))
            r.dX(lambda e: e.dma_start(out=dbg["s_dump"][:], in_=s_spill_d[:]))
            r.barrier()
        if SPARSE:
            emit_ffn_sparse()
        else:
            emit_router()
            r.barrier()
            if debug:
                r.dX(lambda e: e.dma_start(
                    out=dbg["wtok_dump"].transpose([1, 0, 2]),
                    in_=sl(OFF_WTOK, NM * E).rearrange("p (m n) -> p m n", n=E)))
                r.barrier()
            emit_ffn()
        fin_dX = r.cnt["dX"]
        fin_dW = r.cnt["dW"]
        r.rec["gpsimd"].append(lambda EE, SS, _v=fin_dX: EE["gpsimd"].wait_ge(SS["dX"], _v))
        r.rec["sync"].append(lambda EE, SS, _v=fin_dW: EE["sync"].wait_ge(SS["dW"], _v))

        def runner(eng_name):
            def go(eng_obj):
                EE = {eng_name: eng_obj}
                for fn in r.rec[eng_name]:
                    fn(EE, SEMS)
            return go

        block.sync(runner("sync"))
        block.gpsimd(runner("gpsimd"))
        block.vector(runner("vector"))
        block.scalar(runner("scalar"))
        block.tensor(runner("tensor"))

    return nc


def _prep_inputs(L0_attn_x, L0_attn_slope_rate, L0_attn_W_qkvu, L0_attn_W_out,
                 L0_ffn_router_w, L0_ffn_gate_w, L0_ffn_up_w, L0_ffn_down_w):
    x = np.ascontiguousarray(np.asarray(L0_attn_x, np.float32))
    slope = np.ascontiguousarray(np.asarray(L0_attn_slope_rate, np.float32).reshape(H))
    wqkvuT = np.ascontiguousarray(np.asarray(L0_attn_W_qkvu, np.float32).T)
    woutT = np.ascontiguousarray(np.asarray(L0_attn_W_out, np.float32).T)
    wrT = np.ascontiguousarray(np.asarray(L0_ffn_router_w, np.float32).T)
    gate = np.asarray(L0_ffn_gate_w, np.float32)
    up = np.asarray(L0_ffn_up_w, np.float32)
    down = np.asarray(L0_ffn_down_w, np.float32)
    gT = gate.transpose(0, 2, 1).reshape(E, DM, NIC, 128)
    uT = up.transpose(0, 2, 1).reshape(E, DM, NIC, 128)
    wguT = np.ascontiguousarray(
        np.concatenate([gT[:, :, :, None, :], uT[:, :, :, None, :]], axis=3)
        .reshape(E, DM, 2 * I))
    wdwT = np.ascontiguousarray(down.transpose(0, 2, 1))
    zeros = np.zeros((TPC, DM), np.float32)
    in_maps = []
    for c in range(8):
        b, half = c // 2, c % 2
        x_own = np.ascontiguousarray(x[b, half * TPC:(half + 1) * TPC, :])
        x_pre = np.ascontiguousarray(x[b, 0:TPC, :]) if half == 1 else zeros
        in_maps.append(dict(x_own=x_own, x_pre=x_pre, slope=slope, wqkvuT=wqkvuT,
                            woutT=woutT, wrT=wrT, wguT=wguT, wdwT=wdwT))
    return in_maps


def kernel(L0_attn_x, L0_attn_slope_rate, L0_attn_W_qkvu, L0_attn_W_out,
           L0_ffn_router_w, L0_ffn_gate_w, L0_ffn_up_w, L0_ffn_down_w):
    in_maps = _prep_inputs(L0_attn_x, L0_attn_slope_rate, L0_attn_W_qkvu, L0_attn_W_out,
                           L0_ffn_router_w, L0_ffn_gate_w, L0_ffn_up_w, L0_ffn_down_w)
    nc = build_nc(debug=False)
    res = run_bass_kernel_spmd(nc, in_maps, core_ids=list(range(8)))
    out = np.empty((B, N, DM), np.float32)
    for c in range(8):
        b, half = c // 2, c % 2
        out[b, half * TPC:(half + 1) * TPC, :] = res.results[c]["out"]
    return out
